# version 09
# baseline (speedup 1.0000x reference)
import numpy as np
import contextlib
from types import SimpleNamespace

import concourse.bass as bass
import concourse.mybir as mybir

F32 = mybir.dt.float32
BF16 = mybir.dt.bfloat16
I32 = mybir.dt.int32
AF = mybir.ActivationFunctionType
OP = mybir.AluOpType
AX = mybir.AxisListType

H, DH, NPAIR = 16, 64, 8
D = 1024
CH = 64
TB = 128
SEQ = 2048
NBLK = SEQ // TB
NCHB = TB // CH
FFG = 256
HID = 3072
NHC = HID // 128
INV127 = 1.0 / 127.0
INV127SQ = 1.0 / (127.0 * 127.0)


CUR_OP = [None]


class Sched:
    def __init__(self):
        self.ops = []

    def add(self, eng, fn, waits=(), tag=""):
        op = SimpleNamespace(eng=eng, fn=fn, waits=[w for w in waits if w is not None],
                             tag=tag, marked=False, val=None, ndma=0, g_waited=False)
        self.ops.append(op)
        return op

    def gop(self, fn, ndma, waits=(), tag=""):
        op = self.add("g", fn, waits, tag)
        op.ndma = ndma
        return op

    def reorder(self, durs=None):
        durs = durs or {"t": 2000, "v": 1500, "s": 1000, "g": 4000}
        tag_durs = (
            ("gu", 4200), ("dn", 1800), ("pj", 1000), ("rq", 1000), ("rk", 1000),
            ("fl", 1500), ("xtr2", 500), ("xtr", 500), ("ktr", 2000), ("gram", 1000),
            ("dm", 1200), ("sqr", 1000), ("ap", 500), ("om", 1000), ("sm", 500),
            ("rt", 500), ("a0", 500), ("qm", 450), ("tv", 150), ("gc3", 150),
            ("bc", 250), ("kbc", 250), ("obc", 150), ("cs", 500), ("kcs", 500),
            ("ocs", 500), ("wom", 3600), ("cv", 800), ("lred", 600), ("xn", 1100),
            ("lo", 700), ("g", 900), ("ls2", 800), ("knm", 700), ("cb", 1300),
            ("lr", 600), ("rr", 300), ("krr", 300), ("ae", 700), ("w", 700),
            ("wp", 700), ("rhs", 700), ("su", 800), ("onm", 1300), ("oa", 1300),
            ("fa", 1300), ("hpm", 600), ("hlo", 500), ("hist", 300), ("wof", 1000),
            ("neg", 100), ("rst", 500), ("pe", 300), ("si", 450), ("hhi", 450),
            ("lnsq", 1100), ("lsq", 300), ("hilo", 1100), ("sq", 700), ("ksq", 700),
            ("rs", 600), ("krs", 600), ("de", 600), ("qe", 600), ("sqe", 1000),
            ("osq", 1100), ("ors", 300), ("gc5", 250), ("lam", 100), ("flev", 450),
            ("ls1", 1800), ("xnw2", 1100), ("xnw", 1100), ("tve", 150), ("kte", 1100),
            ("oe", 700), ("sbf", 700), ("orr", 200), ("re", 300),
        )
        def op_dur(op):
            base = op.tag.rstrip("0123456789_")
            best = None
            for pre, d in tag_durs:
                if base.startswith(pre) and (best is None or len(pre) > best[0]):
                    best = (len(pre), d)
            return best[1] if best else durs[op.eng]
        import heapq
        n = len(self.ops)
        idx = {id(op): i for i, op in enumerate(self.ops)}
        children = [[] for _ in range(n)]
        indeg = [0] * n
        for i, op in enumerate(self.ops):
            seen = set()
            for w in op.waits:
                j = idx[id(w)]
                if j not in seen:
                    seen.add(j)
                    children[j].append(i)
                    indeg[i] += 1
        last_by_eng = {}
        for i, op in enumerate(self.ops):
            if op.eng == "g":
                j = last_by_eng.get(op.eng)
                if j is not None:
                    children[j].append(i)
                    indeg[i] += 1
                last_by_eng[op.eng] = i
        done_t = [0] * n
        avail = {"t": 0, "v": 0, "s": 0, "g": 0}
        ready = []
        for i, op in enumerate(self.ops):
            if indeg[i] == 0:
                heapq.heappush(ready, (0, i))
        order = []
        while ready:
            pend = []
            best = None
            for _ in range(len(ready)):
                rt, i = heapq.heappop(ready)
                op = self.ops[i]
                st = max(rt, avail[op.eng])
                if best is None or (st, i) < best[0:2]:
                    if best is not None:
                        pend.append((best[2], best[1]))
                    best = (st, i, rt)
                else:
                    pend.append((rt, i))
            for rt, i in pend:
                heapq.heappush(ready, (rt, i))
            st, i, rt = best
            op = self.ops[i]
            fin = st + op_dur(op)
            avail[op.eng] = fin
            done_t[i] = fin
            order.append(op)
            for c in children[i]:
                indeg[c] -= 1
                if indeg[c] == 0:
                    heapq.heappush(ready, (max(done_t[idx[id(w)]] for w in self.ops[c].waits)
                                           if self.ops[c].waits else 0, c))
        assert len(order) == n, (len(order), n)
        self.ops = order

    def finalize(self):
        for op in self.ops:
            for w in op.waits:
                if w.eng != op.eng:
                    w.marked = True
                if op.eng == "g" and w.eng in ("v", "s"):
                    w.g_waited = True
        cnt = {"t": 0, "v": 0, "s": 0, "g": 0}
        for op in self.ops:
            if op.eng == "g":
                cnt["g"] += op.ndma
                op.val = cnt["g"]
            else:
                if op.marked:
                    cnt[op.eng] += 1
                op.val = cnt[op.eng]
        self.final_counts = cnt

    def emit(self, block, sems):
        sched = self

        def mk(eng_name):
            def run(eh):
                for op in sched.ops:
                    if op.eng != eng_name:
                        continue
                    for w in op.waits:
                        if w.eng == op.eng and op.eng != "g":
                            continue
                        eh.wait_ge(sems[w.eng], (16 * w.val) if w.eng == "g" else w.val)
                    CUR_OP[0] = op
                    if op.eng == "g":
                        cnt = [0]
                        def dma(out, in_, **kw):
                            cnt[0] += 1
                            return eh.dma_start(out=out, in_=in_, **kw).then_inc(sems["g"], 16)
                        op.fn(eh, dma)
                        assert cnt[0] == op.ndma, f"{op.tag}: {op.ndma} != {cnt[0]}"
                    else:
                        if op.eng == "v" and any(w.eng == "v" for w in op.waits):
                            eh.drain()
                        last = op.fn(eh)
                        if op.marked:
                            assert last is not None, f"marked op {op.tag} returned None"
                            if op.eng in ("v", "s"):
                                eh.drain().then_inc(sems[op.eng], 1)
                            else:
                                last.then_inc(sems[op.eng], 1)
                CUR_OP[0] = None
                if eng_name == "g":
                    eh.wait_ge(sems["g"], 16 * sched.final_counts["g"])
            return run

        block.tensor(mk("t"))
        block.vector(mk("v"))
        block.scalar(mk("s"))
        block.gpsimd(mk("g"))


def build(stage=6, dbg=(), n_seq=2, n_blk=NBLK, verbose=True, chsub=9, reorder=True):
    nc = bass.Bass("TRN2", target_bir_lowering=False, debug=False,
                   detect_race_conditions=False)
    dbg = set(dbg)

    dtt = lambda n, s, d, k="ExternalInput": nc.dram_tensor(n, s, d, kind=k).ap()
    x_h = dtt("x", [n_seq * SEQ, D], F32)
    wq_h = dtt("wq", [D, D], F32)
    wk_h = dtt("wk", [D, D], F32)
    wv_h = dtt("wv", [D, D], F32)
    wo_h = dtt("wo", [D, D], F32)
    wr2_h = dtt("wr2", [D, 2 * D], F32)
    wfl_h = dtt("wfl", [D, 32], F32)
    bfl_h = dtt("bfl", [1, 32], F32)
    convw_h = dtt("convw", [D, 12], F32)
    qn2_h = dtt("qn2", [128, 1], F32)
    kn2_h = dtt("kn2", [128, 1], F32)
    onw8_h = dtt("onw8", [128, 8], F32)
    gs8_h = dtt("gs8", [128, 8], F32)
    wgT_h = dtt("wgT", [D, HID], BF16)
    wuT_h = dtt("wuT", [D, HID], BF16)
    wdT_h = dtt("wdT", [HID, D], BF16)
    c_ident_h = dtt("c_ident", [128, 128], F32)
    c_maskbh_h = dtt("c_maskbh", [128, 512], F32)
    c_maskb_h = dtt("c_maskb", [128, 512], F32)
    c_maska_h = dtt("c_maska", [128, 512], F32)
    c_ones_h = dtt("c_ones", [128, 512], F32)
    c_cvec_h = dtt("c_cvec", [128, 8], F32)
    out_h = dtt("out", [n_seq * SEQ, D], F32, "ExternalOutput")
    stg_dram = nc.dram_tensor("stg_dram", [16, 8 * CH], F32, kind="Internal").ap()
    gc_dram = nc.dram_tensor("gc_dram", [2, 8], F32, kind="Internal").ap()
    xlo_h = nc.dram_tensor("xlo_scr", [128, 8, SEQ], BF16, kind="Internal").ap()
    wbf_h = nc.dram_tensor("wbf_scr", [4 * D, D], BF16, kind="Internal").ap()
    wrbf_h = nc.dram_tensor("wrbf_scr", [D, 2 * D], BF16, kind="Internal").ap()
    wflbf_h = nc.dram_tensor("wflbf_scr", [D, 32], BF16, kind="Internal").ap()

    dbg_h = {}
    for name, shape in (("xn", [128, 8 * TB]), ("qfm", [128, 8 * TB]), ("kfm", [128, 8 * TB]),
                        ("vfm", [128, 8 * TB]), ("qtil", [128, 8 * TB]), ("ktil", [128, 8 * TB]),
                        ("flq", [128, 7 * TB]), ("gtok", [128, 6 * 8]), ("ofm", [128, 8 * TB]),
                        ("stagerow", [1, 3104]), ("bh", [128, 512]), ("wsl", [128, 512]),
                        ("lnv", [128, 4]), ("xt", [128, 1024]),
                        ("hp", [128, NHC * FFG])):
        if name in dbg:
            dbg_h[name] = dtt("dbg_" + name, shape, F32, "ExternalOutput")

    es = contextlib.ExitStack()
    with es:
        sb = lambda n, s, d, **kw: es.enter_context(nc.sbuf_tensor(n, s, d, **kw))
        ps = lambda n, s, d: es.enter_context(nc.psum_tensor(n, s, d))
        ident = sb("ident", [128, 128], BF16)
        identf32 = sb("identf32", [8, 8], F32)
        maskbh = sb("maskbh", [128, 512], BF16)
        maskb = sb("maskb", [128, 512], BF16)
        maska = sb("maska", [128, 512], BF16)
        onesb = sb("onesb", [128, 128], BF16)
        onesf32 = sb("onesf32", [128, 64], F32)
        negonesf32 = sb("negonesf32", [128, 64], F32)
        cvec = sb("cvecs", [128, 8], F32)
        qn2 = sb("qn2s", [128, 1], F32)
        kn2 = sb("kn2s", [128, 1], F32)
        onw8 = sb("onw8s", [128, 8], F32)
        lam = sb("lam", [128, 8], F32)
        convw = sb("convws", [128, 8, 12], F32)
        bfl = sb("bfls", [1, 32], BF16)
        xn_seq = sb("xn_seq", [128, 8, SEQ], BF16)
        xlo_stage = sb("xlo_stage", [128, 8, TB], BF16)
        xlo_buf = sb("xlo_buf", [128, 1, 8, FFG], BF16)
        wrbuf = sb("wrbuf", [128, 2, 8, 128], BF16)
        S_sb = sb("S_sb", [128, 8, DH], F32)
        S_bf = sb("S_bf", [128, 8, DH], BF16)
        x_slab2 = sb("x_slab2", [128, 2, D], F32)
        out_slab = sb("out_slab", [128, D], F32)
        stage_sb = sb("stage_sb", [1, 2 * 3104], F32)
        stage_pv = sb("stage_pv", [8, 2 * 256], F32)
        lnv = sb("lnv", [128, 8], F32)
        xt_slab = sb("xt_slab", [128, D], F32)
        xh_slab = sb("xh_slab", [128, D], BF16)
        xl_slab = sb("xl_slab", [128, D], BF16)
        sqscr = sb("sqscr", [128, D], BF16)
        hp_hi = sb("hp_hi", [128, 2, FFG], BF16)
        hp_lo = sb("hp_lo", [128, 2, FFG], BF16)
        wdbuf = sb("wdbuf", [128, 2, D], BF16)
        sil = sb("sil", [128, FFG], F32)
        hh32 = sb("hh32", [128, 1, FFG], F32)
        ov = contextlib.ExitStack()
        sbr = lambda n, s, d: ov.enter_context(nc.sbuf_tensor(n, s, d, side="right"))
        wq_sb = sbr("wq_sb", [128, 8, D], BF16)
        wk_sb = sbr("wk_sb", [128, 8, D], BF16)
        wv_sb = sbr("wv_sb", [128, 8, D], BF16)
        wo_sb = sbr("wo_sb", [128, 8, D], BF16)
        wfl_sb = sbr("wfl_sb", [128, 8, 32], BF16)
        qfm = sbr("qfm", [128, 8, TB], BF16)
        kfm = sbr("kfm", [128, 8, TB], BF16)
        vfm = sbr("vfm", [128, 8, TB], BF16)
        rfm = sbr("rfm", [128, 8, TB], BF16)
        ktok = sbr("ktok", [128, 8, CH], BF16)
        vtok = sbr("vtok", [128, 8, CH], BF16)
        ofm = sbr("ofm", [128, 8, TB], BF16)
        flq = sbr("flq", [128, 7, TB], F32)
        tokv = sbr("tokv", [128, 6, 8], F32)
        dscr = sbr("dscr", [128, 512], BF16)
        datile = sbr("datile", [128, 512], BF16)
        bh_sb = sbr("bh_sb", [128, 512], BF16)
        b_sb2 = sbr("b_sb2", [128, 512], BF16)
        bh_nx = sbr("bh_nx", [128, 512], BF16)
        b_nx = sbr("b_nx", [128, 512], BF16)
        qhat = sbr("qhat", [128, 512], BF16)
        vj = [sbr("vj0", [128, 512], BF16), sbr("vj1", [128, 512], BF16)]
        wsl = sbr("wsl", [128, 512], BF16)
        wps = sbr("wps", [128, 512], BF16)
        t2f = wps
        t1f = sbr("t1f", [128, 512], BF16)
        rsc = sbr("rsc", [65, 512], BF16)
        qraw = sbr("qraw", [128, 8, TB + 3], BF16)
        kraw = sbr("kraw", [128, 8, TB + 3], BF16)
        vraw = sbr("vraw", [128, 8, TB + 3], BF16)
        ov.close()
        ovb = contextlib.ExitStack()
        wg_sb = ovb.enter_context(nc.sbuf_tensor("wg_sb", [128, 8, HID], BF16, side="right"))
        wu_sb = ovb.enter_context(nc.sbuf_tensor("wu_sb", [128, 8, HID], BF16, side="right"))
        ovb.close()
        rem = nc.sbuf_bytes_remaining
        pb_gkk = ps("pb_gkk", [128, 512], F32)
        pb_gkq = ps("pb_gkq", [128, 512], F32)
        pb_exp = ps("pb_exp", [128, 512], F32)
        pb_mm = ps("pb_mm", [128, 512], F32)
        pb_p0 = ps("pb_p0", [128, 512], F32)
        pb_p1 = ps("pb_p1", [128, 512], F32)
        ovp = contextlib.ExitStack()
        pb_o = ovp.enter_context(nc.psum_tensor("pb_o", [128, 512], F32))
        pt_tr = ovp.enter_context(nc.psum_tensor("pt_tr", [128, 1024], BF16))
        ovp.close()
        pb_q0 = ps("pb_q0", [128, 512], F32)
        pb_q1 = ps("pb_q1", [128, 512], F32)

        sems = {k: es.enter_context(nc.semaphore("sem_" + k)) for k in "tvsg"}
        for s in sems.values():
            nc.gpsimd.sem_clear(s)
        nc.all_engine_barrier()
        block = es.enter_context(nc.Block())
        sc = Sched()

        def f_cl(eh, dma):
            dma(ident[:], c_ident_h)
            dma(identf32[:], c_ident_h[0:8, 0:8])
            dma(maskbh[:], c_maskbh_h)
            dma(maskb[:], c_maskb_h)
            dma(maska[:], c_maska_h)
            dma(onesb[:], c_ones_h[:, 0:128])
            dma(onesf32[:], c_ones_h[:, 0:64])
            dma(negonesf32[:], c_ones_h[:, 64:128])
            dma(cvec[:], c_cvec_h)
            dma(qn2[:], qn2_h)
            dma(kn2[:], kn2_h)
            dma(onw8[:], onw8_h)
            dma(lam[:], gs8_h)
            dma(convw[:], convw_h.rearrange("(c p) k -> p c k", p=128))
            dma(bfl[:], bfl_h)
        ev_cl = sc.gop(f_cl, 15, (), "consts")
        ev_neg = sc.add("v", lambda eh: eh.tensor_scalar_mul(negonesf32[:], negonesf32[:], -1.0),
                        [ev_cl], "neg")
        ev_lam = sc.add("s", lambda eh: eh.activation(lam[:], lam[:], AF.Abs), [ev_cl], "lam")

        cast_jobs = []
        for wi, w_h in enumerate((wq_h, wk_h, wv_h, wo_h)):
            for rb in range(8):
                for cb in range(4):
                    cast_jobs.append((wbf_h[wi * D + rb * 128:wi * D + (rb + 1) * 128,
                                            cb * 256:(cb + 1) * 256],
                                      w_h[rb * 128:(rb + 1) * 128, cb * 256:(cb + 1) * 256]))
        for rb in range(8):
            for cb in range(8):
                cast_jobs.append((wrbf_h[rb * 128:(rb + 1) * 128, cb * 256:(cb + 1) * 256],
                                  wr2_h[rb * 128:(rb + 1) * 128, cb * 256:(cb + 1) * 256]))
        ev_half = [None, None]
        for j, (dstp, srcp) in enumerate(cast_jobs):
            hb = j % 2
            stg = dscr[:, hb * 256:(hb + 1) * 256]
            ev_c = sc.gop(lambda eh, dma, stg=stg, srcp=srcp: dma(stg, srcp),
                          1, [ev_half[hb]], f"wc{j}")
            ev_half[hb] = sc.gop(lambda eh, dma, dstp=dstp, stg=stg: dma(dstp, stg),
                                 1, [ev_c], f"ws{j}")
        ev_c = sc.gop(lambda eh, dma: dma(dscr[:, 0:256].rearrange("p (c f) -> p c f", f=32),
                                          wfl_h.rearrange("(c p) f -> p c f", p=128)),
                      1, [ev_half[0]], "wcfl")
        ev_half[0] = sc.gop(lambda eh, dma: dma(wflbf_h.rearrange("(c p) f -> p c f", p=128),
                                                dscr[:, 0:256].rearrange("p (c f) -> p c f", f=32)),
                            1, [ev_c], "wsfl")
        ev_cast = [ev_half[0], ev_half[1]]

        prev_b_done_v = None
        prev_b_done_t = None
        prev_seq_fst = None
        wr_prev_use = [None, None]
        ev_wom = None
        ev_ost = None
        ev_dmms = None

        for s_i in range(n_seq):
            def f_wl(eh, dma):
                for wi, w_sb_ in enumerate((wq_sb, wk_sb, wv_sb, wo_sb)):
                    for c in range(8):
                        dma(w_sb_[:, c, :], wbf_h[wi * D + c * 128:wi * D + (c + 1) * 128, :])
                dma(wfl_sb[:], wflbf_h.rearrange("(c p) f -> p c f", p=128))
            if stage >= 2:
                ev_wl = sc.gop(f_wl, 33,
                               [w for w in (prev_b_done_v, prev_b_done_t, ev_cl) if w] + ev_cast,
                               f"wl{s_i}")
            else:
                ev_wl = ev_cl

            def f_wofold(eh):
                last = None
                for c in range(8):
                    last = eh.tensor_scalar_mul(wo_sb[:, c, :], wo_sb[:, c, :], onw8[:, c:c + 1])
                return last
            ev_wofold = (sc.add("v", f_wofold, [ev_wl, ev_cl, ev_neg], f"wof{s_i}")
                         if stage >= 2 else ev_neg)

            def f_rst(eh):
                eh.memset(S_sb[:], 0.0)
                eh.memset(S_bf[:], 0.0)
                eh.memset(flq[:], 0.0)
                eh.memset(tokv[:], 0.0)
                eh.memset(qraw[:, :, 0:3], 0.0)
                eh.memset(kraw[:, :, 0:3], 0.0)
                return eh.memset(vraw[:, :, 0:3], 0.0)
            ev_rst = sc.add("v", f_rst, [prev_b_done_v] if prev_b_done_v else [], f"rst{s_i}")

            xfree_hist = {0: prev_b_done_t, 1: prev_b_done_t}
            ev_ost = prev_seq_fst
            ev_rawfree = None
            ev_Supd = ev_rst
            ev_xnw_last = None
            prev_kte = prev_b_done_t
            prev_omm = None
            prev_oev = None
            prev_qh = None
            prev_qe = None
            prev_smm = None
            prev_smm_box = [None]
            prev_sqr0 = None
            prev_sqe4 = None
            prev_oadd = None
            prev_ap4 = None
            prev_xt_guard = None
            prev_xh_guard = prev_b_done_t
            prev_xl_guard = prev_b_done_t
            prev_xlo_st = None
            stage_used = {0: None, 1: None}
            gci = 0

            for b in range(n_blk):
                t0 = b * TB
                xr0 = s_i * SEQ + t0
                dbg_here = (s_i == 0 and b == 0)
                xb = b % 2
                x_slab = x_slab2[:, xb]

                ev_xl = sc.gop(lambda eh, dma, xr0=xr0, x_slab=x_slab:
                               dma(x_slab[:], x_h[xr0:xr0 + 128, :]),
                               1, [xfree_hist[xb]], f"xl{s_i}_{b}")
                ev_sq = sc.add("s", lambda eh, x_slab=x_slab:
                               eh.activation(xt_slab[:], x_slab[:], AF.Square),
                               [ev_xl, prev_xt_guard], f"lnsq{s_i}_{b}")
                def f_lred(eh, x_slab=x_slab):
                    eh.tensor_reduce(lnv[:, 0:1], x_slab[:], AX.X, OP.add)
                    eh.tensor_reduce(lnv[:, 1:2], xt_slab[:], AX.X, OP.add)
                    eh.drain()
                    eh.tensor_scalar_mul(lnv[:, 2:3], lnv[:, 0:1], 1.0 / D)
                    eh.tensor_scalar_mul(lnv[:, 1:2], lnv[:, 1:2], 1.0 / D)
                    eh.tensor_tensor(lnv[:, 3:4], lnv[:, 2:3], lnv[:, 2:3], OP.mult)
                    eh.drain()
                    return eh.tensor_sub(lnv[:, 1:2], lnv[:, 1:2], lnv[:, 3:4])
                ev_lred = sc.add("v", f_lred, [ev_sq], f"lred{s_i}_{b}")
                def f_lsq(eh):
                    eh.activation(lnv[:, 4:5], lnv[:, 1:2], AF.Ln, bias=cvec[:, 0:1])
                    return eh.activation(lnv[:, 3:4], lnv[:, 4:5], AF.Exp, scale=-0.5)
                ev_lsq = sc.add("s", f_lsq, [ev_lred], f"lsq{s_i}_{b}")
                def f_xn(eh, x_slab=x_slab):
                    return eh.tensor_scalar(xt_slab[:], x_slab[:], lnv[:, 2:3], lnv[:, 3:4],
                                            OP.subtract, OP.mult)
                ev_xn = sc.add("v", f_xn, [ev_lsq], f"xn{s_i}_{b}")
                ev_hilo = sc.add("s", lambda eh: eh.activation(xh_slab[:], xt_slab[:], AF.Copy),
                                 [ev_xn, prev_xh_guard], f"hilo{s_i}_{b}")
                ev_lo = sc.add("v", lambda eh: eh.tensor_sub(xl_slab[:], xt_slab[:], xh_slab[:]),
                               [ev_hilo, prev_xl_guard], f"lo{s_i}_{b}")
                def f_xtr(eh):
                    last = None
                    for c in range(8):
                        last = eh.transpose(pt_tr[:, c * 128:(c + 1) * 128],
                                            xh_slab[:, c * 128:(c + 1) * 128], ident[:])
                    return last
                ev_xtr = sc.add("t", f_xtr, [ev_hilo, ev_cl, prev_kte], f"xtr{s_i}_{b}")
                def f_xnw(eh, t0=t0):
                    return eh.activation(xn_seq[:, :, t0:t0 + TB],
                                         pt_tr[:].rearrange("p (c t) -> p c t", c=8), AF.Copy)
                ev_xnw = sc.add("s", f_xnw, [ev_xtr], f"xnw{s_i}_{b}")
                def f_xtr2(eh):
                    last = None
                    for c in range(8):
                        last = eh.transpose(pt_tr[:, c * 128:(c + 1) * 128],
                                            xl_slab[:, c * 128:(c + 1) * 128], ident[:])
                    return last
                ev_xtr2 = sc.add("t", f_xtr2, [ev_lo, ev_xnw], f"xtr2{s_i}_{b}")
                def f_xnw2(eh):
                    return eh.activation(xlo_stage[:],
                                         pt_tr[:].rearrange("p (c t) -> p c t", c=8), AF.Copy)
                ev_xnw2 = sc.add("s", f_xnw2, [ev_xtr2, prev_xlo_st], f"xnw2{s_i}_{b}")
                ev_xlo_st = sc.gop(lambda eh, dma, t0=t0:
                                   dma(xlo_h[:, :, t0:t0 + TB], xlo_stage[:]),
                                   1, [ev_xnw2], f"xlst{s_i}_{b}")
                prev_xlo_st = ev_xlo_st
                prev_xh_guard = ev_xtr
                prev_xl_guard = ev_xtr2
                ev_xnw_last = ev_xnw2
                if dbg_here and "xn" in dbg:
                    sc.gop(lambda eh, dma: dma(dbg_h["xn"].rearrange("p (c t) -> p c t", t=TB),
                                               xn_seq[:, :, 0:TB]),
                           1, [ev_xnw], "dbgxn")

                if dbg_here and "lnv" in dbg:
                    sc.gop(lambda eh, dma: dma(dbg_h["lnv"][:], lnv[:]), 1, [ev_xn],
                           "dbglnv")
                if dbg_here and "xt" in dbg:
                    sc.gop(lambda eh, dma: dma(dbg_h["xt"][:], xt_slab[:]), 1, [ev_xn],
                           "dbgxt")
                if stage < 2:
                    xfree_hist[xb] = ev_xn
                    prev_xt_guard = ev_lo
                    continue
                if b > 0:
                    def f_hist(eh):
                        eh.tensor_copy(qraw[:, :, 0:3], qraw[:, :, TB:TB + 3])
                        eh.tensor_copy(kraw[:, :, 0:3], kraw[:, :, TB:TB + 3])
                        return eh.tensor_copy(vraw[:, :, 0:3], vraw[:, :, TB:TB + 3])
                    ev_hist = sc.add("v", f_hist, [ev_rawfree], f"hist{s_i}_{b}")
                else:
                    ev_hist = ev_rst

                ev_conv = {}
                prev_evac = {}
                for mi, (w_sb_, raw, fmt) in enumerate(((wq_sb, qraw, qfm), (wk_sb, kraw, kfm),
                                                        (wv_sb, vraw, vfm))):
                    for m in range(8):
                        idx = mi * 8 + m
                        pbank = pb_p0 if idx % 2 == 0 else pb_p1
                        def f_pmm(eh, w_sb_=w_sb_, m=m, t0=t0, pbank=pbank):
                            last = None
                            for d in range(8):
                                last = eh.matmul(pbank[:, 0:TB],
                                                 w_sb_[:, d, m * 128:(m + 1) * 128],
                                                 xn_seq[:, d, t0:t0 + TB],
                                                 start=(d == 0), stop=(d == 7))
                            return last
                        waits = [ev_xnw, ev_wl]
                        if idx >= 2:
                            waits.append(prev_evac[idx - 2])
                        else:
                            waits.append(prev_oadd)
                        ev_pm = sc.add("t", f_pmm, waits, f"pj{s_i}_{b}_{mi}_{m}")
                        def f_pev(eh, raw=raw, m=m, pbank=pbank):
                            return eh.activation(raw[:, m, 3:3 + TB], pbank[:, 0:TB], AF.Copy)
                        ev_pe = sc.add("s", f_pev, [ev_pm, ev_hist], f"pe{s_i}_{b}_{mi}_{m}")
                        prev_evac[idx] = ev_pe
                        def f_cv(eh, raw=raw, fmt=fmt, m=m, mi=mi):
                            w = convw[:, m, mi * 4:mi * 4 + 4]
                            eh.tensor_scalar_mul(t1f[:, 0:TB], raw[:, m, 0:TB], w[:, 0:1])
                            eh.scalar_tensor_tensor(t1f[:, 0:TB], raw[:, m, 1:1 + TB], w[:, 1:2],
                                                    t1f[:, 0:TB], OP.mult, OP.add)
                            eh.scalar_tensor_tensor(t1f[:, 0:TB], raw[:, m, 2:2 + TB], w[:, 2:3],
                                                    t1f[:, 0:TB], OP.mult, OP.add)
                            return eh.scalar_tensor_tensor(fmt[:, m, :], raw[:, m, 3:3 + TB],
                                                           w[:, 3:4], t1f[:, 0:TB],
                                                           OP.mult, OP.add)
                        ev_conv[(mi, m)] = sc.add("v", f_cv, [ev_pe, prev_omm],
                                                  f"cv{s_i}_{b}_{mi}_{m}")
                ev_rawfree = ev_conv[(2, 7)]
                if dbg_here:
                    for nm, tile in (("qfm", qfm), ("kfm", kfm), ("vfm", vfm)):
                        if nm in dbg:
                            sc.gop(lambda eh, dma, nm=nm, tile=tile:
                                   dma(dbg_h[nm][:], tile[:].rearrange("p c t -> p (c t)")),
                                   1, [ev_conv[(2, 7)]], "dbg" + nm)
                if stage < 3:
                    xfree_hist[xb] = ev_xn
                    prev_xt_guard = ev_lo
                    continue

                def f_flmm(eh, t0=t0):
                    last = None
                    for li in range(2):
                        colbase = li * TB
                        for half in range(2):
                            wcol = li * 16 + half * 8
                            part0 = 0 if half == 0 else 64
                            pos = (0, 0) if half == 0 else (0, 64)
                            for d in range(8):
                                last = eh.matmul(pb_exp[part0:part0 + 8, colbase:colbase + TB],
                                                 wfl_sb[:, d, wcol:wcol + 8],
                                                 xn_seq[:, d, t0:t0 + TB],
                                                 start=(d == 0), stop=False, tile_position=pos)
                            last = eh.matmul(pb_exp[part0:part0 + 8, colbase:colbase + TB],
                                             bfl[0:1, wcol:wcol + 8], onesb[0:1, 0:TB],
                                             start=False, stop=True, tile_position=pos)
                    return last
                ev_fl = sc.add("t", f_flmm, [ev_xnw, ev_wl, ev_cl, prev_qe], f"fl{s_i}_{b}")
                def f_flev(eh):
                    eh.activation(flq[0:8, 0, :], pb_exp[0:8, 0:TB], AF.Copy)
                    eh.activation(flq[64:72, 0, :], pb_exp[64:72, 0:TB], AF.Copy)
                    eh.activation(flq[0:8, 1, :], pb_exp[0:8, TB:2 * TB], AF.Copy)
                    return eh.activation(flq[64:72, 1, :], pb_exp[64:72, TB:2 * TB], AF.Copy)
                ev_flev = sc.add("s", f_flev, [ev_fl], f"flev{s_i}_{b}")
                def f_ls1(eh):
                    eh.activation(flq[:, 2, :], flq[:, 0, :], AF.Abs)
                    eh.activation(flq[:, 2, :], flq[:, 2, :], AF.Exp, scale=-1.0)
                    eh.activation(flq[:, 2, :], flq[:, 2, :], AF.Ln, bias=cvec[:, 3:4])
                    eh.activation(flq[:, 3, :], flq[:, 1, :], AF.Abs)
                    eh.activation(flq[:, 3, :], flq[:, 3, :], AF.Exp, scale=-1.0)
                    eh.activation(flq[:, 3, :], flq[:, 3, :], AF.Ln, bias=cvec[:, 3:4])
                    return eh.activation(flq[:, 6, :], flq[:, 1, :], AF.Sigmoid)
                ev_ls1 = sc.add("s", f_ls1, [ev_flev], f"ls1{s_i}_{b}")
                def f_ls2(eh):
                    eh.tensor_scalar_min(flq[:, 4, :], flq[:, 0, :], 0.0)
                    eh.tensor_sub(flq[:, 2, :], flq[:, 4, :], flq[:, 2, :])
                    eh.tensor_scalar_min(flq[:, 4, :], flq[:, 1, :], 0.0)
                    return eh.tensor_sub(flq[:, 3, :], flq[:, 4, :], flq[:, 3, :])
                ev_ls2 = sc.add("v", f_ls2, [ev_ls1], f"ls2{s_i}_{b}")
                def f_g(eh):
                    for c in range(NCHB):
                        sl = slice(c * CH, (c + 1) * CH)
                        eh.tensor_tensor_scan(flq[:, 4, sl], onesf32[:, 0:CH], flq[:, 2, sl],
                                              0.0, OP.mult, OP.add)
                    eh.drain()
                    eh.tensor_sub(flq[:, 0, :], flq[:, 4, :], flq[:, 2, :])
                    return eh.tensor_sub(flq[:, 5, :], flq[:, 3, :], flq[:, 4, :])
                ev_g = sc.add("v", f_g, [ev_ls2], f"g{s_i}_{b}")
                if dbg_here and "flq" in dbg:
                    sc.gop(lambda eh, dma: dma(dbg_h["flq"][:],
                                               flq[:].rearrange("p c t -> p (c t)")),
                           1, [ev_g], "dbgflq")

                def emit_tilde(which, ev_deps):
                    mi_src = 0 if which == "q" else 1
                    fm_t = qfm if which == "q" else kfm
                    wcolbase = 0 if which == "q" else D
                    def fm_fat(tile, hsl):
                        return tile[:, :, hsl]
                    def r3(flat):
                        return flat.rearrange("p (c t) -> p c t", t=CH)
                    evs_re = []
                    for m in range(8):
                        pbank = pb_p0 if m % 2 == 0 else pb_p1
                        mb = m % 2
                        def f_wrl(eh, dma, m=m, mb=mb, wcolbase=wcolbase):
                            colb = wcolbase + m * 128
                            dma(wrbuf[:, mb], wrbf_h[:, colb:colb + 128]
                                .rearrange("(c p) w -> p c w", p=128))
                        ev_wrl = sc.gop(f_wrl, 1, [wr_prev_use[mb], ev_wl],
                                        f"wrl{which}{s_i}_{b}_{m}")
                        def f_rmm(eh, t0=t0, pbank=pbank, mb=mb):
                            last = None
                            for d in range(8):
                                last = eh.matmul(
                                    pbank[:, TB:2 * TB],
                                    wrbuf[:, mb, d, :],
                                    xn_seq[:, d, t0:t0 + TB],
                                    start=(d == 0), stop=(d == 7))
                            return last
                        waits = [ev_xnw, ev_wrl] + (list(ev_deps) if m < 2 else [])
                        if m >= 2:
                            waits.append(evs_re[m - 2])
                        if m < 2:
                            waits.append(prev_evac[22 + (m % 2)])
                        ev_rm = sc.add("t", f_rmm, waits, f"r{which}{s_i}_{b}_{m}")
                        wr_prev_use[mb] = ev_rm
                        def f_rev(eh, m=m, pbank=pbank):
                            return eh.activation(rfm[:, m, :], pbank[:, TB:2 * TB], AF.Copy)
                        ev_re = sc.add("s", f_rev,
                                       [ev_rm] + (list(ev_deps) if m < 2 else []),
                                       f"re{which}{s_i}_{b}_{m}")
                        evs_re.append(ev_re)
                    ev_out = None
                    for hb in range(2):
                        hsl = slice(hb * CH, (hb + 1) * CH)
                        sq_waits = [ev_conv[(mi_src, 7)], ev_xnw, ev_out]
                        if hb == 0:
                            sq_waits += list(ev_deps)
                        ev_sq2 = sc.add("s", lambda eh, fm_t=fm_t, hsl=hsl:
                                        eh.activation(r3(sqscr[:, 0:512]), fm_fat(fm_t, hsl),
                                                      AF.Square),
                                        sq_waits, f"sq{which}{s_i}_{b}_{hb}")
                        def f_cs(eh):
                            last = None
                            for p in range(8):
                                last = eh.matmul(pb_exp[0:1, p * CH:(p + 1) * CH],
                                                 onesb[0:64, 0:1],
                                                 sqscr[0:64, p * CH:(p + 1) * CH],
                                                 start=True, stop=True)
                                last = eh.matmul(pb_exp[64:65, p * CH:(p + 1) * CH],
                                                 onesb[64:128, 0:1],
                                                 sqscr[64:128, p * CH:(p + 1) * CH],
                                                 start=True, stop=True, tile_position=(64, 64))
                            return last
                        ev_cs = sc.add("t", f_cs, [ev_sq2, ev_flev], f"cs{which}{s_i}_{b}_{hb}")
                        def f_rs(eh):
                            eh.activation(rsc[0:1, 0:512], pb_exp[0:1, 0:512], AF.Sqrt,
                                          bias=cvec[0:1, 1:2], scale=1.0 / DH)
                            return eh.activation(rsc[64:65, 0:512], pb_exp[64:65, 0:512],
                                                 AF.Sqrt, bias=cvec[64:65, 1:2],
                                                 scale=1.0 / DH)
                        ev_rs = sc.add("s", f_rs, [ev_cs], f"rs{which}{s_i}_{b}_{hb}")
                        def f_rr(eh):
                            with nc.allow_low_precision(reason="bf16 rstd feeds bf16 matmul"):
                                eh.reciprocal(rsc[0:1, 0:512], rsc[0:1, 0:512])
                                return eh.reciprocal(rsc[64:65, 0:512], rsc[64:65, 0:512])
                        ev_rr = sc.add("v", f_rr, [ev_rs], f"rr{which}{s_i}_{b}_{hb}")
                        def f_bc(eh):
                            eh.matmul(pb_exp[0:64, 0:512], onesb[0:1, 0:64], rsc[0:1, 0:512],
                                      start=True, stop=True)
                            return eh.matmul(pb_exp[64:128, 0:512], onesb[64:65, 0:64],
                                             rsc[64:65, 0:512], start=True, stop=True,
                                             tile_position=(64, 64))
                        ev_bc = sc.add("t", f_bc, [ev_rr], f"bc{which}{s_i}_{b}_{hb}")
                        def f_lr(eh, hsl=hsl):
                            last = None
                            for p in range(8):
                                last = eh.tensor_scalar_mul(t2f[:, p * CH:(p + 1) * CH],
                                                            rfm[:, p, hsl], lam[:, p:p + 1])
                            return last
                        ev_lr = sc.add("v", f_lr, [evs_re[7], ev_lam, prev_smm_box[0]], f"lr{which}{s_i}_{b}_{hb}")
                        nw = qn2 if which == "q" else kn2
                        def f_cb(eh, fm_t=fm_t, hsl=hsl, nw=nw):
                            eh.tensor_tensor(r3(t1f[:, 0:512]), fm_fat(fm_t, hsl),
                                             r3(pb_exp[:, 0:512]), OP.mult)
                            return eh.scalar_tensor_tensor(fm_fat(fm_t, hsl), r3(t1f[:, 0:512]),
                                                           nw[:, 0:1], r3(t2f[:, 0:512]),
                                                           OP.mult, OP.add)
                        ev_cb = sc.add("v", f_cb, [ev_bc, ev_lr, prev_omm],
                                       f"cb{which}{s_i}_{b}_{hb}")
                        if which == "k":
                            ev_sq3 = sc.add("s", lambda eh, hsl=hsl:
                                            eh.activation(r3(sqscr[:, 0:512]), fm_fat(kfm, hsl),
                                                          AF.Square),
                                            [ev_cb], f"ksq{s_i}_{b}_{hb}")
                            ev_cs2 = sc.add("t", f_cs, [ev_sq3], f"kcs{s_i}_{b}_{hb}")
                            def f_rs2(eh):
                                eh.activation(rsc[0:1, 0:512], pb_exp[0:1, 0:512], AF.Sqrt)
                                return eh.activation(rsc[64:65, 0:512], pb_exp[64:65, 0:512],
                                                     AF.Sqrt)
                            ev_rs2 = sc.add("s", f_rs2, [ev_cs2], f"krs{s_i}_{b}_{hb}")
                            def f_rr2(eh):
                                eh.tensor_scalar_max(rsc[0:1, 0:512], rsc[0:1, 0:512], 1e-12)
                                eh.tensor_scalar_max(rsc[64:65, 0:512], rsc[64:65, 0:512], 1e-12)
                                with nc.allow_low_precision(reason="bf16 rstd feeds bf16 matmul"):
                                    eh.reciprocal(rsc[0:1, 0:512], rsc[0:1, 0:512])
                                    return eh.reciprocal(rsc[64:65, 0:512], rsc[64:65, 0:512])
                            ev_rr2 = sc.add("v", f_rr2, [ev_rs2], f"krr{s_i}_{b}_{hb}")
                            ev_bc2 = sc.add("t", f_bc, [ev_rr2], f"kbc{s_i}_{b}_{hb}")
                            ev_cb = sc.add("v", lambda eh, hsl=hsl:
                                           eh.tensor_tensor(fm_fat(kfm, hsl), fm_fat(kfm, hsl),
                                                            r3(pb_exp[:, 0:512]), OP.mult),
                                           [ev_bc2], f"knm{s_i}_{b}_{hb}")
                        ev_out = ev_cb
                    return ev_out

                ev_qt = emit_tilde("q", [ev_wom] if ev_wom else [])
                ev_kt = emit_tilde("k", [ev_qt])
                if dbg_here:
                    for nm, tile, evd in (("qtil", qfm, ev_qt), ("ktil", kfm, ev_kt)):
                        if nm in dbg:
                            sc.gop(lambda eh, dma, nm=nm, tile=tile:
                                   dma(dbg_h[nm][:], tile[:].rearrange("p c t -> p (c t)")),
                                   1, [evd], "dbg" + nm)
                if stage < 4:
                    xfree_hist[xb] = ev_xn
                    prev_xt_guard = ev_lo
                    continue

                ev_oev = prev_oev
                for c in range(NCHB):
                    csl = slice(c * CH, (c + 1) * CH)
                    dbg_ch = dbg_here and c == 0
                    cp = gci % 2
                    gci += 1
                    so = cp * 3104
                    po = cp * 256
                    go = cp * 4 * CH

                    def f_stg1(eh, dma, csl=csl, go=go):
                        dma(stg_dram[0:8, go + 0:go + CH], flq[0:8, 4, csl])
                        dma(stg_dram[8:16, go + 0:go + CH], flq[64:72, 4, csl])
                        dma(stg_dram[0:8, go + CH:go + 2 * CH], flq[0:8, 0, csl])
                        dma(stg_dram[8:16, go + CH:go + 2 * CH], flq[64:72, 0, csl])
                        dma(stg_dram[0:8, go + 2 * CH:go + 3 * CH], flq[0:8, 5, csl])
                        dma(stg_dram[8:16, go + 2 * CH:go + 3 * CH], flq[64:72, 5, csl])
                        dma(stg_dram[0:8, go + 3 * CH:go + 4 * CH], flq[0:8, 6, csl])
                        dma(stg_dram[8:16, go + 3 * CH:go + 4 * CH], flq[64:72, 6, csl])
                    ev_st1 = sc.gop(f_stg1, 8, [ev_g, stage_used[cp]], f"st1{s_i}_{b}_{c}")
                    def f_stg2(eh, dma, so=so, po=po, go=go):
                        for q in range(3):
                            dma(stage_sb[0:1, so + q * 1024:so + (q + 1) * 1024]
                                .rearrange("o (r t) -> o r t", t=CH),
                                stg_dram[:, go + q * CH:go + (q + 1) * CH]
                                .rearrange("(o r) t -> o r t", o=1))
                        for j, q in enumerate((0, 3)):
                            dma(stage_pv[0:8, po + j * 128:po + (j + 1) * 128]
                                .rearrange("p (hf t) -> p hf t", hf=2),
                                stg_dram[:, go + q * CH:go + (q + 1) * CH]
                                .rearrange("(hf p) t -> p hf t", hf=2))
                    ev_st2 = sc.gop(f_stg2, 5, [ev_st1], f"st2{s_i}_{b}_{c}")
                    if dbg_ch and "stagerow" in dbg:
                        sc.gop(lambda eh, dma: dma(dbg_h["stagerow"][:], stage_sb[:, 0:3104]),
                               1, [ev_st2], "dbgst")

                    def f_tv(eh, po=po):
                        eh.matmul(pb_mm[:, 0:8], stage_pv[0:8, po:po + 128], identf32[0:8, 0:8],
                                  start=True, stop=True, skip_group_check=True)
                        return eh.matmul(pb_mm[:, 8:16], stage_pv[0:8, po + 128:po + 256],
                                         identf32[0:8, 0:8], start=True, stop=True,
                                         skip_group_check=True)
                    if chsub < 2:
                        stage_used[cp] = ev_st2
                        continue
                    ev_tv = sc.add("t", f_tv, [ev_st2, ev_g, ev_kt, ev_Supd], f"tv{s_i}_{b}_{c}")
                    def f_tve(eh):
                        eh.activation(tokv[:, 0, :], pb_mm[:, 0:8], AF.Copy)
                        return eh.activation(tokv[:, 1, :], pb_mm[:, 8:16], AF.Copy)
                    ev_tve = sc.add("s", f_tve, [ev_tv], f"tve{s_i}_{b}_{c}")
                    def f_gc3(eh, so=so):
                        eh.matmul(pb_mm[0:64, 16:24], onesf32[0:1, 0:64],
                                  stage_sb[0:1, so + 63:so + 512:64], start=True, stop=True,
                                  skip_group_check=True)
                        return eh.matmul(pb_mm[64:128, 16:24], onesf32[0:1, 0:64],
                                         stage_sb[0:1, so + 575:so + 1024:64], start=True,
                                         stop=True, tile_position=(0, 64),
                                         skip_group_check=True)
                    ev_gc3 = sc.add("t", f_gc3, [ev_st2, ev_tve], f"gc3{s_i}_{b}_{c}")
                    def f_gc4(eh):
                        eh.tensor_copy(tokv[:, 2, :], pb_mm[:, 16:24])
                        eh.drain()
                        return eh.tensor_sub(tokv[:, 4, :], tokv[:, 2, :], tokv[:, 0, :])
                    ev_gc4 = sc.add("v", f_gc4, [ev_gc3], f"gc4{s_i}_{b}_{c}")
                    def f_gc5(eh):
                        eh.activation(tokv[:, 4, :], tokv[:, 4, :], AF.Exp)
                        return eh.activation(tokv[:, 3, :], tokv[:, 2, :], AF.Exp)
                    ev_gc5 = sc.add("s", f_gc5, [ev_gc4], f"gc5{s_i}_{b}_{c}")
                    if dbg_ch and "gtok" in dbg:
                        sc.gop(lambda eh, dma: dma(dbg_h["gtok"][:],
                                                   tokv[:].rearrange("p c t -> p (c t)")),
                               1, [ev_gc5], "dbggtok")

                    def f_ktr(eh, csl=csl):
                        last = None
                        for p in range(8):
                            last = eh.transpose(pt_tr[0:64, p * CH:(p + 1) * CH],
                                                kfm[0:64, p, csl], ident[0:64, 0:64])
                            last = eh.transpose(pt_tr[64:128, p * CH:(p + 1) * CH],
                                                kfm[64:128, p, csl], ident[64:128, 64:128],
                                                tile_position=(64, 64))
                            last = eh.transpose(pt_tr[0:64, 512 + p * CH:512 + (p + 1) * CH],
                                                vfm[0:64, p, csl], ident[0:64, 0:64])
                            last = eh.transpose(pt_tr[64:128, 512 + p * CH:512 + (p + 1) * CH],
                                                vfm[64:128, p, csl], ident[64:128, 64:128],
                                                tile_position=(64, 64))
                        return last
                    ev_ktr = sc.add("t", f_ktr, [ev_kt, ev_conv[(2, 7)], ev_xnw2, prev_kte],
                                    f"ktr{s_i}_{b}_{c}")
                    def f_kte(eh):
                        eh.activation(ktok[:].rearrange("p c t -> p (c t)"), pt_tr[:, 0:512],
                                      AF.Copy)
                        return eh.activation(vtok[:].rearrange("p c t -> p (c t)"),
                                             pt_tr[:, 512:1024], AF.Copy)
                    ev_kte = sc.add("s", f_kte, [ev_ktr, prev_smm], f"kte{s_i}_{b}_{c}")
                    prev_kte = ev_kte

                    if chsub < 3:
                        stage_used[cp] = ev_gc3
                        continue
                    def f_gram(eh, csl=csl):
                        last = None
                        for p in range(8):
                            cc = slice(p * CH, (p + 1) * CH)
                            last = eh.matmul(pb_gkk[0:64, cc], kfm[0:64, p, csl],
                                             kfm[0:64, p, csl], start=True, stop=True,
                                             tile_position=(0, 0))
                            last = eh.matmul(pb_gkk[64:128, cc], kfm[64:128, p, csl],
                                             kfm[64:128, p, csl], start=True, stop=True,
                                             tile_position=(64, 64))
                            last = eh.matmul(pb_gkq[0:64, cc], kfm[0:64, p, csl],
                                             qfm[0:64, p, csl], start=True, stop=True,
                                             tile_position=(0, 0))
                            last = eh.matmul(pb_gkq[64:128, cc], kfm[64:128, p, csl],
                                             qfm[64:128, p, csl], start=True, stop=True,
                                             tile_position=(64, 64))
                        return last
                    ev_gram = sc.add("t", f_gram, [ev_kt, ev_qt, prev_sqe4],
                                     f"gram{s_i}_{b}_{c}")

                    def decay(dm_waits, dt_waits, mask_t, qrow, qpp, ppneg, outtile, grambank,
                              tag):
                        def f_dm(eh, mask_t=mask_t, qrow=qrow, qpp=qpp, ppneg=ppneg, so=so):
                            eh.matmul(pb_exp[:, 0:512], ident[:], mask_t[:, 0:512],
                                      start=True, stop=False, skip_group_check=True)
                            eh.matmul(pb_exp[0:64, 0:512], onesf32[0:1, 0:64],
                                      stage_sb[0:1, so + qrow * 1024:so + qrow * 1024 + 512],
                                      start=False, stop=False, skip_group_check=True)
                            eh.matmul(pb_exp[64:128, 0:512], onesf32[0:1, 0:64],
                                      stage_sb[0:1, so + qrow * 1024 + 512:so + qrow * 1024 + 1024],
                                      start=False, stop=False, tile_position=(0, 64),
                                      skip_group_check=True)
                            last = None
                            rhs1 = negonesf32 if ppneg else onesf32
                            for p in range(8):
                                base = so + qpp * 1024
                                last = eh.matmul(
                                    pb_exp[0:64, p * CH:(p + 1) * CH],
                                    stage_sb[0:1, base + p * CH:base + (p + 1) * CH],
                                    rhs1[0:1, 0:CH], start=False, stop=True,
                                    skip_group_check=True)
                                last = eh.matmul(
                                    pb_exp[64:128, p * CH:(p + 1) * CH],
                                    stage_sb[0:1, base + 512 + p * CH:base + 512 + (p + 1) * CH],
                                    rhs1[0:1, 0:CH], start=False, stop=True,
                                    tile_position=(0, 64), skip_group_check=True)
                            return last
                        ev_dm = sc.add("t", f_dm, dm_waits, f"dm{tag}")
                        ev_de = sc.add("s", lambda eh: eh.activation(dscr[:, 0:512],
                                                                     pb_exp[:, 0:512], AF.Exp),
                                       [ev_dm], f"de{tag}")
                        def f_dt(eh, outtile=outtile, grambank=grambank):
                            return eh.tensor_tensor(outtile[:, 0:512], dscr[:, 0:512],
                                                    grambank[:, 0:512], OP.mult)
                        ev_dt = sc.add("v", f_dt, [ev_de, ev_gram] + list(dt_waits), f"dt{tag}")
                        return ev_dm, ev_de, ev_dt

                    ev_dm1, ev_de1, ev_bh = decay([ev_gram, ev_st2, ev_gc3, ev_cl, ev_kt, prev_qe],
                                                  [prev_sqr0], maskbh, 1, 2, False, bh_sb,
                                                  pb_gkk, f"bh{s_i}_{b}_{c}")
                    ev_dm2, ev_de2, ev_b2 = decay([ev_bh], [prev_sqr0], maskb, 2, 1, False,
                                                  b_sb2, pb_gkk, f"b{s_i}_{b}_{c}")
                    ev_dm3, ev_de3, ev_da = decay([ev_b2], [prev_omm], maska, 0, 0, True,
                                                  datile, pb_gkq, f"da{s_i}_{b}_{c}")
                    if dbg_ch and "bh" in dbg:
                        sc.gop(lambda eh, dma: dma(dbg_h["bh"][:], bh_sb[:]), 1, [ev_bh], "dbgbh")

                    def f_qm(eh, so=so):
                        eh.matmul(pb_exp[0:64, 0:512], onesf32[0:1, 0:64],
                                  stage_sb[0:1, so + 0:so + 512], start=True, stop=True,
                                  skip_group_check=True)
                        return eh.matmul(pb_exp[64:128, 0:512], onesf32[0:1, 0:64],
                                         stage_sb[0:1, so + 512:so + 1024],
                                         start=True, stop=True, tile_position=(0, 64),
                                         skip_group_check=True)
                    ev_qm = sc.add("t", f_qm, [ev_da], f"qm{s_i}_{b}_{c}")
                    ev_qe = sc.add("s", lambda eh: eh.activation(dscr[:, 0:512], pb_exp[:, 0:512],
                                                                 AF.Exp),
                                   [ev_qm, ev_da, prev_qh], f"qe{s_i}_{b}_{c}")
                    prev_qe = ev_qe
                    def f_qh(eh, csl=csl):
                        return eh.tensor_tensor(qhat[:, 0:512].rearrange("p (c t) -> p c t", t=CH),
                                                dscr[:, 0:512].rearrange("p (c t) -> p c t", t=CH),
                                                qfm[:, :, csl], OP.mult)
                    ev_qh = sc.add("v", f_qh, [ev_qe, ev_qt, prev_omm], f"qh{s_i}_{b}_{c}")
                    prev_qh = ev_qh

                    if chsub < 4:
                        stage_used[cp] = ev_qm
                        continue
                    def f_retr(eh, csl=csl):
                        last = None
                        for p in range(8):
                            cc = slice(p * CH, (p + 1) * CH)
                            last = eh.matmul(pb_mm[0:64, cc], kfm[0:64, p, csl],
                                             S_bf[0:64, p, :], start=True, stop=True,
                                             tile_position=(0, 0))
                            last = eh.matmul(pb_mm[64:128, cc], kfm[64:128, p, csl],
                                             S_bf[64:128, p, :], start=True, stop=True,
                                             tile_position=(64, 64))
                        return last
                    ev_retr = sc.add("t", f_retr, [ev_kt, ev_Supd, ev_tve, ev_gc4],
                                     f"rt{s_i}_{b}_{c}")
                    def f_rhs(eh):
                        return eh.tensor_sub(vj[0][:, 0:512],
                                             vtok[:].rearrange("p c t -> p (c t)"),
                                             pb_mm[:, 0:512])
                    ev_rhs = sc.add("v", f_rhs, [ev_retr, ev_kte], f"rhs{s_i}_{b}_{c}")

                    def f_ap0(eh):
                        last = None
                        for p in range(8):
                            cc = slice(p * CH, (p + 1) * CH)
                            last = eh.matmul(pb_mm[0:64, cc], bh_sb[0:64, cc], vj[0][0:64, cc],
                                             start=True, stop=True, tile_position=(0, 0))
                            last = eh.matmul(pb_mm[64:128, cc], bh_sb[64:128, cc],
                                             vj[0][64:128, cc], start=True, stop=True,
                                             tile_position=(64, 64))
                        return last
                    ev_a0 = sc.add("t", f_ap0, [ev_bh, ev_rhs], f"a0{s_i}_{b}_{c}")
                    ev_vj = sc.add("v", lambda eh: eh.tensor_sub(vj[1][:, 0:512], vj[0][:, 0:512],
                                                                 pb_mm[:, 0:512]),
                                   [ev_a0], f"ae0{s_i}_{b}_{c}")
                    vcur = 1
                    cur_bh, cur_b = bh_sb, b_sb2
                    nxt_bh, nxt_b = bh_nx, b_nx
                    ev_pw = [ev_bh, ev_b2, ev_da]
                    for lev in range(5):
                        lastlev = (lev == 4)
                        def f_sqr(eh, cur_bh=cur_bh, cur_b=cur_b, lastlev=lastlev):
                            last = None
                            for p in range(8):
                                cc = slice(p * CH, (p + 1) * CH)
                                a, bb = slice(0, 64), slice(64, 128)
                                last = eh.matmul(pb_gkk[a, cc], cur_b[a, cc], cur_bh[a, cc],
                                                 start=True, stop=True, tile_position=(0, 0))
                                last = eh.matmul(pb_gkk[bb, cc], cur_b[bb, cc], cur_bh[bb, cc],
                                                 start=True, stop=True, tile_position=(64, 64))
                                if not lastlev:
                                    last = eh.matmul(pb_gkq[a, cc], cur_bh[a, cc], cur_b[a, cc],
                                                     start=True, stop=True, tile_position=(0, 0))
                                    last = eh.matmul(pb_gkq[bb, cc], cur_bh[bb, cc],
                                                     cur_b[bb, cc], start=True, stop=True,
                                                     tile_position=(64, 64))
                            return last
                        sqr_waits = list(ev_pw)
                        if lev == 0:
                            sqr_waits += [prev_ap4]
                        ev_sqr = sc.add("t", f_sqr, sqr_waits, f"sqr{s_i}_{b}_{c}_{lev}")
                        if lev == 0:
                            prev_sqr0 = ev_sqr
                        def f_sqe(eh, nxt_bh=nxt_bh, nxt_b=nxt_b, lastlev=lastlev):
                            r = eh.activation(nxt_bh[:, 0:512], pb_gkk[:, 0:512], AF.Copy)
                            if lastlev:
                                return r
                            return eh.activation(nxt_b[:, 0:512], pb_gkq[:, 0:512], AF.Copy)
                        ev_sqe = sc.add("s", f_sqe, [ev_sqr], f"sqe{s_i}_{b}_{c}_{lev}")
                        if lastlev:
                            prev_sqe4 = ev_sqe
                        def f_ap(eh, nxt_bh=nxt_bh, vcur=vcur):
                            last = None
                            for p in range(8):
                                cc = slice(p * CH, (p + 1) * CH)
                                last = eh.matmul(pb_mm[0:64, cc], nxt_bh[0:64, cc],
                                                 vj[vcur][0:64, cc], start=True, stop=True,
                                                 tile_position=(0, 0))
                                last = eh.matmul(pb_mm[64:128, cc], nxt_bh[64:128, cc],
                                                 vj[vcur][64:128, cc], start=True, stop=True,
                                                 tile_position=(64, 64))
                            return last
                        ev_ap = sc.add("t", f_ap, [ev_sqe, ev_vj], f"ap{s_i}_{b}_{c}_{lev}")
                        if lastlev:
                            prev_ap4 = ev_ap
                        def f_ae(eh, vcur=vcur):
                            return eh.tensor_add(vj[1 - vcur][:, 0:512], vj[vcur][:, 0:512],
                                                 pb_mm[:, 0:512])
                        ev_vj = sc.add("v", f_ae, [ev_ap], f"ae{s_i}_{b}_{c}_{lev}")
                        vcur = 1 - vcur
                        cur_bh, nxt_bh = nxt_bh, cur_bh
                        cur_b, nxt_b = nxt_b, cur_b
                        ev_pw = [ev_sqe, ev_vj]
                    if chsub < 5:
                        stage_used[cp] = ev_qm
                        continue
                    def f_w(eh, vcur=vcur):
                        last = None
                        for p in range(8):
                            cc = slice(p * CH, (p + 1) * CH)
                            last = eh.tensor_scalar_mul(wsl[:, cc], vj[vcur][:, cc],
                                                        tokv[:, 1, p:p + 1])
                        return last
                    ev_w = sc.add("v", f_w, [ev_vj, ev_tve, prev_omm], f"w{s_i}_{b}_{c}")
                    if dbg_ch and "wsl" in dbg:
                        sc.gop(lambda eh, dma: dma(dbg_h["wsl"][:], wsl[:]), 1, [ev_w], "dbgw")

                    def f_omm(eh):
                        last = None
                        for p in range(8):
                            cc = slice(p * CH, (p + 1) * CH)
                            last = eh.matmul(pb_o[0:64, cc], S_bf[0:64, p, :], qhat[0:64, cc],
                                             start=True, stop=False, tile_position=(0, 0))
                            last = eh.matmul(pb_o[0:64, cc], wsl[0:64, cc], datile[0:64, cc],
                                             start=False, stop=True, tile_position=(0, 0))
                            last = eh.matmul(pb_o[64:128, cc], S_bf[64:128, p, :],
                                             qhat[64:128, cc], start=True, stop=False,
                                             tile_position=(64, 64))
                            last = eh.matmul(pb_o[64:128, cc], wsl[64:128, cc],
                                             datile[64:128, cc], start=False, stop=True,
                                             tile_position=(64, 64))
                        return last
                    ev_omm = sc.add("t", f_omm, [ev_w, ev_qh, ev_da, prev_oev], f"om{s_i}_{b}_{c}")
                    prev_omm = ev_omm
                    def f_oev(eh, csl=csl):
                        return eh.activation(ofm[:, :, csl],
                                             pb_o[:, 0:512].rearrange("p (c t) -> p c t", t=CH),
                                             AF.Copy)
                    ev_oev = sc.add("s", f_oev, [ev_omm], f"oe{s_i}_{b}_{c}")
                    prev_oev = ev_oev

                    def f_wp(eh):
                        last = None
                        for p in range(8):
                            cc = slice(p * CH, (p + 1) * CH)
                            last = eh.tensor_scalar_mul(wps[:, cc], wsl[:, cc],
                                                        tokv[:, 4, p:p + 1])
                        return last
                    ev_wp = sc.add("v", f_wp, [ev_w, ev_gc5, prev_smm], f"wp{s_i}_{b}_{c}")
                    def f_smm(eh):
                        last = None
                        for p in range(8):
                            cc = slice(p * CH, (p + 1) * CH)
                            last = eh.matmul(pb_mm[0:64, cc], ktok[0:64, p, :], wps[0:64, cc],
                                             start=True, stop=True, tile_position=(0, 0))
                            last = eh.matmul(pb_mm[64:128, cc], ktok[64:128, p, :],
                                             wps[64:128, cc], start=True, stop=True,
                                             tile_position=(64, 64))
                        return last
                    ev_smm = sc.add("t", f_smm, [ev_wp, ev_kte, ev_omm], f"sm{s_i}_{b}_{c}")
                    prev_smm = ev_smm
                    prev_smm_box[0] = ev_smm
                    def f_sup(eh):
                        last = None
                        for p in range(8):
                            cc = slice(p * CH, (p + 1) * CH)
                            last = eh.scalar_tensor_tensor(S_sb[:, p, :], S_sb[:, p, :],
                                                           tokv[:, 3, p:p + 1], pb_mm[:, cc],
                                                           OP.mult, OP.add)
                        return last
                    ev_sup0 = sc.add("v", f_sup, [ev_smm, ev_gc5], f"su{s_i}_{b}_{c}")
                    ev_Supd = sc.add("s", lambda eh: eh.activation(S_bf[:], S_sb[:], AF.Copy),
                                     [ev_sup0], f"sbf{s_i}_{b}_{c}")
                    stage_used[cp] = ev_qm

                if dbg_here and "ofm" in dbg:
                    sc.gop(lambda eh, dma: dma(dbg_h["ofm"][:],
                                               ofm[:].rearrange("p c t -> p (c t)")),
                           1, [ev_oev], "dbgofm")
                if stage < 5:
                    xfree_hist[xb] = ev_xn
                    prev_xt_guard = ev_lo
                    continue

                ev_osq = sc.add("s", lambda eh: eh.activation(sqscr[:, 0:D],
                                                              ofm[:].rearrange("p c t -> p (c t)"),
                                                              AF.Square),
                                [ev_oev, ev_qe], f"osq{s_i}_{b}")
                def f_ocs(eh):
                    last = None
                    for cc2 in range(8):
                        last = eh.matmul(pb_exp[0:1, 0:TB], onesb[:, 0:1],
                                         sqscr[:, cc2 * TB:(cc2 + 1) * TB],
                                         start=(cc2 == 0), stop=(cc2 == 7))
                    return last
                ev_ocs = sc.add("t", f_ocs, [ev_osq, ev_qm], f"ocs{s_i}_{b}")
                ev_ors = sc.add("s", lambda eh: eh.activation(rsc[0:1, 0:TB], pb_exp[0:1, 0:TB],
                                                              AF.Sqrt, bias=cvec[0:1, 1:2],
                                                              scale=1.0 / D),
                                [ev_ocs], f"ors{s_i}_{b}")
                def f_orr(eh):
                    with nc.allow_low_precision(reason="bf16 rstd feeds bf16 matmul"):
                        return eh.reciprocal(rsc[0:1, 0:TB], rsc[0:1, 0:TB])
                ev_orr = sc.add("v", f_orr, [ev_ors], f"orr{s_i}_{b}")
                ev_obc = sc.add("t", lambda eh: eh.matmul(pb_exp[:, 0:TB], onesb[0:1, 0:128],
                                                          rsc[0:1, 0:TB], start=True, stop=True),
                                [ev_orr], f"obc{s_i}_{b}")
                def f_onm(eh):
                    last = None
                    for cc2 in range(8):
                        last = eh.tensor_tensor(rfm[:, cc2, :], ofm[:, cc2, :], pb_exp[:, 0:TB],
                                                OP.mult)
                    return last
                ev_onm = sc.add("v", f_onm, [ev_obc], f"onm{s_i}_{b}")
                def f_wom(eh):
                    last = None
                    for half in range(2):
                        pbank = pb_p0 if half == 0 else pb_p1
                        for cc2 in range(8):
                            last = eh.matmul(pbank[:, 0:512], rfm[:, cc2, :],
                                             wo_sb[:, cc2, half * 512:(half + 1) * 512],
                                             start=(cc2 == 0), stop=(cc2 == 7))
                    return last
                ev_wom = sc.add("t", f_wom, [ev_onm, ev_wofold], f"wom{s_i}_{b}")
                def f_oadd(eh, x_slab=x_slab):
                    eh.tensor_add(out_slab[:, 0:512], pb_p0[:, 0:512], x_slab[:, 0:512])
                    return eh.tensor_add(out_slab[:, 512:1024], pb_p1[:, 0:512],
                                         x_slab[:, 512:1024])
                ev_oadd = sc.add("v", f_oadd, [ev_wom, ev_ost], f"oa{s_i}_{b}")
                prev_oadd = ev_oadd
                ev_ost = sc.gop(lambda eh, dma, xr0=xr0: dma(out_h[xr0:xr0 + 128, :],
                                                             out_slab[:]),
                                1, [ev_oadd], f"os{s_i}_{b}")
                xfree_hist[xb] = ev_ost
                prev_xt_guard = ev_lo

            if stage < 6:
                prev_b_done_v = ev_Supd
                prev_b_done_t = None
                continue

            def f_bwl(eh, dma):
                for c in range(8):
                    dma(wg_sb[:, c, :], wgT_h[c * 128:(c + 1) * 128, :])
                    dma(wu_sb[:, c, :], wuT_h[c * 128:(c + 1) * 128, :])
            ev_bwl = sc.gop(f_bwl, 16, [ev_wom, ev_Supd, ev_ost], f"bwl{s_i}")

            ev_fadd_prev = None
            ev_fst_prev = prev_seq_fst
            hp_evs = {}
            dn_evs = {}
            hhi_evs = {}
            hpm_evs = {}
            gu_last_of_gr = {}
            xlo_ld = {}
            ev_dn_last = None
            ngr = (n_blk * TB) // FFG
            for gr in range(ngr):
                g0 = gr * FFG
                def f_xll(eh, dma, g0=g0):
                    dma(xlo_buf[:, 0], xlo_h[:, :, g0:g0 + FFG])
                xlo_ld[gr] = sc.gop(f_xll, 1,
                                    [prev_xlo_st, ev_bwl, gu_last_of_gr.get(gr - 1)],
                                    f"xll{s_i}_{gr}")
                for hc in range(NHC):
                    gi = gr * NHC + hc
                    hb = hc % 2
                    gbank = pb_gkk if hb == 0 else pb_gkq
                    ubank = pb_exp if hb == 0 else pb_mm
                    def f_gu(eh, hc=hc, g0=g0, gr=gr, gbank=gbank, ubank=ubank):
                        last = None
                        for bank, w_sb_ in ((gbank, wg_sb), (ubank, wu_sb)):
                            for d in range(8):
                                last = eh.matmul(bank[:, 0:FFG],
                                                 w_sb_[:, d, hc * 128:(hc + 1) * 128],
                                                 xn_seq[:, d, g0:g0 + FFG],
                                                 start=(d == 0), stop=False)
                            for d in range(8):
                                last = eh.matmul(bank[:, 0:FFG],
                                                 w_sb_[:, d, hc * 128:(hc + 1) * 128],
                                                 xlo_buf[:, 0, d, :],
                                                 start=False, stop=(d == 7))
                        return last
                    waits = [ev_bwl, ev_xnw_last, hp_evs.get(gi - 2), xlo_ld[gr]]
                    ev_gu = sc.add("t", f_gu, waits, f"gu{s_i}_{gr}_{hc}")
                    gu_last_of_gr[gr] = ev_gu
                    ev_si = sc.add("s", lambda eh, gbank=gbank: eh.activation(
                        sil[:, 0:FFG], gbank[:, 0:FFG], AF.Silu, scale=INV127),
                        [ev_gu, hpm_evs.get(gi - 1)], f"si{s_i}_{gr}_{hc}")
                    def f_hp(eh, ubank=ubank):
                        return eh.scalar_tensor_tensor(hh32[:, 0, :], ubank[:, 0:FFG],
                                                       INV127SQ, sil[:, 0:FFG],
                                                       OP.mult, OP.mult)
                    ev_hpm = sc.add("v", f_hp, [ev_si, hhi_evs.get(gi - 1)],
                                    f"hpm{s_i}_{gr}_{hc}")
                    hpm_evs[gi] = ev_hpm
                    ev_hhi = sc.add("s", lambda eh, hb=hb: eh.activation(
                        hp_hi[:, hb, :], hh32[:, 0, :], AF.Copy),
                        [ev_hpm, dn_evs.get(gi - 2)], f"hhi{s_i}_{gr}_{hc}")
                    hhi_evs[gi] = ev_hhi
                    def f_hlo(eh, hb=hb):
                        return eh.tensor_sub(hp_lo[:, hb, :], hh32[:, 0, :], hp_hi[:, hb, :])
                    ev_hlo = sc.add("v", f_hlo, [ev_hhi, dn_evs.get(gi - 2)],
                                    f"hlo{s_i}_{gr}_{hc}")
                    hp_evs[gi] = ev_hlo
                    def f_wdl(eh, dma, hc=hc, hb=hb):
                        dma(wdbuf[:, hb, :], wdT_h[hc * 128:(hc + 1) * 128, :])
                    ev_wdl = sc.gop(f_wdl, 1, [dn_evs.get(gi - 2), ev_bwl],
                                    f"wdl{s_i}_{gr}_{hc}")
                    def f_dn(eh, hc=hc, hb=hb):
                        last = None
                        for pi, hp_t in enumerate((hp_hi, hp_lo)):
                            st = (hc == 0 and pi == 0)
                            sp = (hc == NHC - 1 and pi == 1)
                            for bank, ssl in ((pb_p0, slice(0, 128)), (pb_q0, slice(128, 256))):
                                last = eh.matmul(bank[:, 0:512], hp_t[:, hb, ssl],
                                                 wdbuf[:, hb, 0:512], start=st, stop=sp)
                            for bank, ssl in ((pb_p1, slice(0, 128)), (pb_q1, slice(128, 256))):
                                last = eh.matmul(bank[:, 0:512], hp_t[:, hb, ssl],
                                                 wdbuf[:, hb, 512:1024], start=st, stop=sp)
                        return last
                    dn_waits = [ev_wdl, ev_hlo]
                    if hc == 0:
                        dn_waits.append(ev_fadd_prev if ev_fadd_prev else ev_bwl)
                    ev_dn = sc.add("t", f_dn, dn_waits, f"dn{s_i}_{gr}_{hc}")
                    dn_evs[gi] = ev_dn
                    ev_dn_last = ev_dn
                for sl_i in range(FFG // 128):
                    xr0 = s_i * SEQ + gr * FFG + sl_i * 128
                    ba, bb2 = (pb_p0, pb_p1) if sl_i == 0 else (pb_q0, pb_q1)
                    ev_opl = sc.gop(lambda eh, dma, xr0=xr0: dma(out_slab[:],
                                                                 out_h[xr0:xr0 + 128, :]),
                                    1, [ev_fadd_prev, ev_fst_prev, ev_bwl],
                                    f"opl{s_i}_{gr}_{sl_i}")
                    def f_fadd(eh, ba=ba, bb2=bb2):
                        eh.tensor_add(out_slab[:, 0:512], ba[:, 0:512], out_slab[:, 0:512])
                        return eh.tensor_add(out_slab[:, 512:1024], bb2[:, 0:512],
                                             out_slab[:, 512:1024])
                    ev_fadd = sc.add("v", f_fadd, [ev_dn_last, ev_opl], f"fa{s_i}_{gr}_{sl_i}")
                    ev_fst = sc.gop(lambda eh, dma, xr0=xr0: dma(out_h[xr0:xr0 + 128, :],
                                                                 out_slab[:]),
                                    1, [ev_fadd], f"fs{s_i}_{gr}_{sl_i}")
                    ev_fadd_prev = ev_fadd
                    ev_fst_prev = ev_fst

            prev_b_done_v = ev_fadd_prev
            prev_b_done_t = ev_dn_last
            prev_seq_fst = ev_fst_prev

        if reorder:
            sc.reorder()
        sc.finalize()
        sc.emit(block, sems)
        nc._mk_sched = sc
        if verbose:
            print(f"[mk] ops={len(sc.ops)} counts={sc.final_counts} sbuf_rem={rem}")
    return nc


def prep_inputs(inp, core, n_cores=8, n_seq=2):
    P = {}
    x = inp["L0_attn_x"]
    b0 = core * n_seq
    P["x"] = np.ascontiguousarray(x[b0:b0 + n_seq].reshape(n_seq * SEQ, D))
    P["wq"] = np.ascontiguousarray(inp["L0_attn_Wq"])
    P["wk"] = np.ascontiguousarray(inp["L0_attn_Wk"])
    P["wv"] = np.ascontiguousarray(inp["L0_attn_Wv"])
    P["wo"] = np.ascontiguousarray(inp["L0_attn_Wo"])
    wr2 = np.zeros((D, 2 * D), np.float32)
    wr2r = wr2.reshape(D, 2, H, DH)
    wr2r[:, 0, :, 0:48] = inp["L0_attn_Wrq"].reshape(D, H, 48)
    wr2r[:, 1, :, 0:48] = inp["L0_attn_Wrk"].reshape(D, H, 48)
    P["wr2"] = np.ascontiguousarray(wr2)
    perm = [2 * i for i in range(8)] + [2 * i + 1 for i in range(8)]
    wfl = np.concatenate([inp["L0_attn_Wf"][:, perm], inp["L0_attn_Wl"][:, perm]], axis=1)
    P["wfl"] = np.ascontiguousarray(wfl.astype(np.float32))
    bfl = np.concatenate([np.asarray(inp["L0_attn_bf"])[perm],
                          np.asarray(inp["L0_attn_bl"])[perm]])
    P["bfl"] = np.ascontiguousarray(bfl[None, :].astype(np.float32))
    P["convw"] = np.ascontiguousarray(np.concatenate(
        [inp["L0_attn_conv_q"], inp["L0_attn_conv_k"], inp["L0_attn_conv_v"]], axis=1))
    P["qn2"] = np.ascontiguousarray(np.tile(inp["L0_attn_qn_w"], 2)[:, None].astype(np.float32))
    P["kn2"] = np.ascontiguousarray(np.tile(inp["L0_attn_kn_w"], 2)[:, None].astype(np.float32))
    P["onw8"] = np.ascontiguousarray(
        np.asarray(inp["L0_attn_on_w"]).reshape(8, 128).T.astype(np.float32))
    gs = np.asarray(inp["L0_attn_graph_scale"])
    gs8 = np.empty((128, 8), np.float32)
    for p in range(8):
        gs8[0:64, p] = gs[2 * p]
        gs8[64:128, p] = gs[2 * p + 1]
    P["gs8"] = gs8
    import ml_dtypes
    P["wgT"] = np.ascontiguousarray(inp["L0_ffn_w_gate"].T.astype(ml_dtypes.bfloat16))
    P["wuT"] = np.ascontiguousarray(inp["L0_ffn_w_up"].T.astype(ml_dtypes.bfloat16))
    P["wdT"] = np.ascontiguousarray(inp["L0_ffn_w_down"].T.astype(ml_dtypes.bfloat16))
    P["c_ident"] = np.eye(128, dtype=np.float32)
    tt = np.arange(CH)
    m = np.where(tt[:, None] < tt[None, :], 0.0, -1e30).astype(np.float32)
    P["c_maskbh"] = np.ascontiguousarray(np.tile(m, (2, 8)))
    mb = np.where(tt[:, None] > tt[None, :], 0.0, -1e30).astype(np.float32)
    P["c_maskb"] = np.ascontiguousarray(np.tile(mb, (2, 8)))
    ma = np.where(tt[:, None] <= tt[None, :], 0.0, -1e30).astype(np.float32)
    P["c_maska"] = np.ascontiguousarray(np.tile(ma, (2, 8)))
    P["c_ones"] = np.ones((128, 512), np.float32)
    cv = np.zeros((128, 8), np.float32)
    cv[:, 0] = 1e-5
    cv[:, 1] = 1e-6
    cv[:, 2] = 1e-12
    cv[:, 3] = 1.0
    P["c_cvec"] = cv
    return P


def kernel(**inputs):
    from concourse.bass_utils import run_bass_kernel_spmd
    n_cores = 8
    percore = [prep_inputs(inputs, core) for core in range(n_cores)]
    nc = build(verbose=False)
    r = run_bass_kernel_spmd(nc, percore, core_ids=list(range(n_cores)))
    outs = [np.asarray(r.results[c]["out"], np.float32).reshape(2, SEQ, D)
            for c in range(n_cores)]
    return np.concatenate(outs, axis=0)


# version 11 (final)
# speedup vs baseline: 1.0635x; 1.0635x over previous
import numpy as np
import contextlib
from types import SimpleNamespace

import concourse.bass as bass
import concourse.mybir as mybir

F32 = mybir.dt.float32
BF16 = mybir.dt.bfloat16
I32 = mybir.dt.int32
AF = mybir.ActivationFunctionType
OP = mybir.AluOpType
AX = mybir.AxisListType

H, DH, NPAIR = 16, 64, 8
D = 1024
CH = 64
TB = 128
SEQ = 2048
NBLK = SEQ // TB
NCHB = TB // CH
FFG = 256
HID = 3072
NHC = HID // 128
INV127 = 1.0 / 127.0
INV127SQ = 1.0 / (127.0 * 127.0)


CUR_OP = [None]


class Sched:
    def __init__(self):
        self.ops = []

    def add(self, eng, fn, waits=(), tag=""):
        op = SimpleNamespace(eng=eng, fn=fn, waits=[w for w in waits if w is not None],
                             tag=tag, marked=False, val=None, ndma=0, g_waited=False)
        self.ops.append(op)
        return op

    def gop(self, fn, ndma, waits=(), tag=""):
        op = self.add("g", fn, waits, tag)
        op.ndma = ndma
        return op

    def reorder(self, durs=None):
        durs = durs or {"t": 2000, "v": 1500, "s": 1000, "g": 4000}
        tag_durs = (
            ("gu", 4200), ("dn", 1800), ("pj", 1000), ("rq", 1000), ("rk", 1000),
            ("fl", 1500), ("xtr2", 500), ("xtr", 500), ("ktr", 2000), ("gram", 1000),
            ("dm", 1200), ("sqr", 1000), ("ap", 500), ("om", 1000), ("sm", 500),
            ("rt", 500), ("a0", 500), ("qm", 450), ("tv", 150), ("gc3", 150),
            ("bc", 250), ("kbc", 250), ("obc", 150), ("cs", 500), ("kcs", 500),
            ("ocs", 500), ("wom", 3600), ("cv", 800), ("lred", 600), ("xn", 1100),
            ("lo", 700), ("g", 900), ("ls2", 800), ("knm", 700), ("cb", 1300),
            ("lr", 600), ("rr", 300), ("krr", 300), ("ae", 700), ("w", 700),
            ("wp", 700), ("rhs", 700), ("su", 800), ("onm", 1300), ("oa", 1300),
            ("fa", 1300), ("hpm", 600), ("hlo", 500), ("hist", 300), ("wof", 1000),
            ("neg", 100), ("rst", 500), ("pe", 300), ("si", 450), ("hhi", 450),
            ("lnsq", 1100), ("lsq", 300), ("hilo", 1100), ("sq", 700), ("ksq", 700),
            ("rs", 600), ("krs", 600), ("de", 600), ("qe", 600), ("sqe", 1000),
            ("osq", 1100), ("ors", 300), ("gc5", 250), ("lam", 100), ("flev", 450),
            ("ls1", 1800), ("xnw2", 1100), ("xnw", 1100), ("tve", 150), ("kte", 1100),
            ("oe", 700), ("sbf", 700), ("orr", 200), ("re", 300),
        )
        def op_dur(op):
            base = op.tag.rstrip("0123456789_")
            best = None
            for pre, d in tag_durs:
                if base.startswith(pre) and (best is None or len(pre) > best[0]):
                    best = (len(pre), d)
            return best[1] if best else durs[op.eng]
        import heapq
        n = len(self.ops)
        idx = {id(op): i for i, op in enumerate(self.ops)}
        children = [[] for _ in range(n)]
        indeg = [0] * n
        for i, op in enumerate(self.ops):
            seen = set()
            for w in op.waits:
                j = idx[id(w)]
                if j not in seen:
                    seen.add(j)
                    children[j].append(i)
                    indeg[i] += 1
        last_by_eng = {}
        for i, op in enumerate(self.ops):
            if op.eng == "g":
                j = last_by_eng.get(op.eng)
                if j is not None:
                    children[j].append(i)
                    indeg[i] += 1
                last_by_eng[op.eng] = i
        done_t = [0] * n
        avail = {"t": 0, "v": 0, "s": 0, "g": 0}
        ready = []
        for i, op in enumerate(self.ops):
            if indeg[i] == 0:
                heapq.heappush(ready, (0, i))
        order = []
        while ready:
            pend = []
            best = None
            for _ in range(len(ready)):
                rt, i = heapq.heappop(ready)
                op = self.ops[i]
                st = max(rt, avail[op.eng])
                if best is None or (st, i) < best[0:2]:
                    if best is not None:
                        pend.append((best[2], best[1]))
                    best = (st, i, rt)
                else:
                    pend.append((rt, i))
            for rt, i in pend:
                heapq.heappush(ready, (rt, i))
            st, i, rt = best
            op = self.ops[i]
            fin = st + durs[op.eng]
            avail[op.eng] = fin
            done_t[i] = fin
            order.append(op)
            for c in children[i]:
                indeg[c] -= 1
                if indeg[c] == 0:
                    heapq.heappush(ready, (max(done_t[idx[id(w)]] for w in self.ops[c].waits)
                                           if self.ops[c].waits else 0, c))
        assert len(order) == n, (len(order), n)
        self.ops = order

    def finalize(self):
        for op in self.ops:
            for w in op.waits:
                if w.eng != op.eng:
                    w.marked = True
                if op.eng == "g" and w.eng in ("v", "s"):
                    w.g_waited = True
        cnt = {"t": 0, "v": 0, "s": 0, "g": 0}
        for op in self.ops:
            if op.eng == "g":
                cnt["g"] += op.ndma
                op.val = cnt["g"]
            else:
                if op.marked:
                    cnt[op.eng] += 1
                op.val = cnt[op.eng]
        self.final_counts = cnt

    def emit(self, block, sems):
        sched = self

        v_pos = {}
        k = 0
        for op in self.ops:
            if op.eng == "v":
                v_pos[id(op)] = k
                k += 1

        def mk(eng_name):
            def run(eh):
                for op in sched.ops:
                    if op.eng != eng_name:
                        continue
                    for w in op.waits:
                        if w.eng == op.eng and op.eng != "g":
                            continue
                        eh.wait_ge(sems[w.eng], (16 * w.val) if w.eng == "g" else w.val)
                    CUR_OP[0] = op
                    if op.eng == "g":
                        cnt = [0]
                        def dma(out, in_, **kw):
                            cnt[0] += 1
                            return eh.dma_start(out=out, in_=in_, **kw).then_inc(sems["g"], 16)
                        op.fn(eh, dma)
                        assert cnt[0] == op.ndma, f"{op.tag}: {op.ndma} != {cnt[0]}"
                    else:
                        if op.eng == "v" and any(
                                w.eng == "v"
                                and v_pos[id(op)] - v_pos[id(w)] <= 2
                                for w in op.waits):
                            eh.drain()
                        last = op.fn(eh)
                        if op.marked:
                            assert last is not None, f"marked op {op.tag} returned None"
                            if op.eng in ("v", "s"):
                                eh.drain().then_inc(sems[op.eng], 1)
                            else:
                                last.then_inc(sems[op.eng], 1)
                CUR_OP[0] = None
                if eng_name == "g":
                    eh.wait_ge(sems["g"], 16 * sched.final_counts["g"])
            return run

        block.tensor(mk("t"))
        block.vector(mk("v"))
        block.scalar(mk("s"))
        block.gpsimd(mk("g"))


def build(stage=6, dbg=(), n_seq=2, n_blk=NBLK, verbose=True, chsub=9, reorder=True):
    nc = bass.Bass("TRN2", target_bir_lowering=False, debug=False,
                   detect_race_conditions=False)
    dbg = set(dbg)

    dtt = lambda n, s, d, k="ExternalInput": nc.dram_tensor(n, s, d, kind=k).ap()
    x_h = dtt("x", [n_seq * SEQ, D], F32)
    wq_h = dtt("wq", [D, D], F32)
    wk_h = dtt("wk", [D, D], F32)
    wv_h = dtt("wv", [D, D], F32)
    wo_h = dtt("wo", [D, D], F32)
    wr2_h = dtt("wr2", [D, 2 * D], F32)
    wfl_h = dtt("wfl", [D, 32], F32)
    bfl_h = dtt("bfl", [1, 32], F32)
    convw_h = dtt("convw", [D, 12], F32)
    qn2_h = dtt("qn2", [128, 1], F32)
    kn2_h = dtt("kn2", [128, 1], F32)
    onw8_h = dtt("onw8", [128, 8], F32)
    gs8_h = dtt("gs8", [128, 8], F32)
    wgT_h = dtt("wgT", [D, HID], BF16)
    wuT_h = dtt("wuT", [D, HID], BF16)
    wdT_h = dtt("wdT", [HID, D], BF16)
    c_ident_h = dtt("c_ident", [128, 128], F32)
    c_maskbh_h = dtt("c_maskbh", [128, 512], F32)
    c_maskb_h = dtt("c_maskb", [128, 512], F32)
    c_maska_h = dtt("c_maska", [128, 512], F32)
    c_ones_h = dtt("c_ones", [128, 512], F32)
    c_cvec_h = dtt("c_cvec", [128, 8], F32)
    out_h = dtt("out", [n_seq * SEQ, D], F32, "ExternalOutput")
    stg_dram = nc.dram_tensor("stg_dram", [16, 8 * CH], F32, kind="Internal").ap()
    gc_dram = nc.dram_tensor("gc_dram", [2, 8], F32, kind="Internal").ap()
    xlo_h = nc.dram_tensor("xlo_scr", [128, 8, SEQ], BF16, kind="Internal").ap()
    wbf_h = nc.dram_tensor("wbf_scr", [4 * D, D], BF16, kind="Internal").ap()
    wrbf_h = nc.dram_tensor("wrbf_scr", [D, 2 * D], BF16, kind="Internal").ap()
    wflbf_h = nc.dram_tensor("wflbf_scr", [D, 32], BF16, kind="Internal").ap()

    dbg_h = {}
    for name, shape in (("xn", [128, 8 * TB]), ("qfm", [128, 8 * TB]), ("kfm", [128, 8 * TB]),
                        ("vfm", [128, 8 * TB]), ("qtil", [128, 8 * TB]), ("ktil", [128, 8 * TB]),
                        ("flq", [128, 7 * TB]), ("gtok", [128, 6 * 8]), ("ofm", [128, 8 * TB]),
                        ("stagerow", [1, 3104]), ("bh", [128, 512]), ("wsl", [128, 512]),
                        ("lnv", [128, 4]), ("xt", [128, 1024]),
                        ("hp", [128, NHC * FFG])):
        if name in dbg:
            dbg_h[name] = dtt("dbg_" + name, shape, F32, "ExternalOutput")

    es = contextlib.ExitStack()
    with es:
        sb = lambda n, s, d, **kw: es.enter_context(nc.sbuf_tensor(n, s, d, **kw))
        ps = lambda n, s, d: es.enter_context(nc.psum_tensor(n, s, d))
        ident = sb("ident", [128, 128], BF16)
        identf32 = sb("identf32", [8, 8], F32)
        maskbh = sb("maskbh", [128, 512], BF16)
        maskb = sb("maskb", [128, 512], BF16)
        maska = sb("maska", [128, 512], BF16)
        onesb = sb("onesb", [128, 128], BF16)
        onesf32 = sb("onesf32", [128, 64], F32)
        negonesf32 = sb("negonesf32", [128, 64], F32)
        cvec = sb("cvecs", [128, 8], F32)
        qn2 = sb("qn2s", [128, 1], F32)
        kn2 = sb("kn2s", [128, 1], F32)
        onw8 = sb("onw8s", [128, 8], F32)
        lam = sb("lam", [128, 8], F32)
        convw = sb("convws", [128, 8, 12], F32)
        bfl = sb("bfls", [1, 32], BF16)
        xn_seq = sb("xn_seq", [128, 8, SEQ], BF16)
        xlo_stage = sb("xlo_stage", [128, 8, TB], BF16)
        xlo_buf = sb("xlo_buf", [128, 1, 8, FFG], BF16)
        wrbuf = sb("wrbuf", [128, 3, 8, 128], BF16)
        S_sb = sb("S_sb", [128, 8, DH], F32)
        S_bf = sb("S_bf", [128, 8, DH], BF16)
        x_slab2 = sb("x_slab2", [128, 2, D], F32)
        out_slab = sb("out_slab", [128, D], F32)
        stage_sb = sb("stage_sb", [1, 2 * 3104], F32)
        stage_pv = sb("stage_pv", [8, 2 * 256], F32)
        lnv = sb("lnv", [128, 8], F32)
        xt_slab = sb("xt_slab", [128, D], F32)
        xh_slab = sb("xh_slab", [128, D], BF16)
        xl_slab = sb("xl_slab", [128, D], BF16)
        sqscr = sb("sqscr", [128, D], BF16)
        hp_hi = sb("hp_hi", [128, 2, FFG], BF16)
        hp_lo = sb("hp_lo", [128, 2, FFG], BF16)
        wdbuf = sb("wdbuf", [128, 2, D], BF16)
        sil = sb("sil", [128, FFG], F32)
        hh32 = sb("hh32", [128, 1, FFG], F32)
        ov = contextlib.ExitStack()
        sbr = lambda n, s, d: ov.enter_context(nc.sbuf_tensor(n, s, d, side="right"))
        wq_sb = sbr("wq_sb", [128, 8, D], BF16)
        wk_sb = sbr("wk_sb", [128, 8, D], BF16)
        wv_sb = sbr("wv_sb", [128, 8, D], BF16)
        wo_sb = sbr("wo_sb", [128, 8, D], BF16)
        wfl_sb = sbr("wfl_sb", [128, 8, 32], BF16)
        qfm = sbr("qfm", [128, 8, TB], BF16)
        kfm = sbr("kfm", [128, 8, TB], BF16)
        vfm = sbr("vfm", [128, 8, TB], BF16)
        rfm = sbr("rfm", [128, 8, TB], BF16)
        ktok = sbr("ktok", [128, 8, CH], BF16)
        vtok = sbr("vtok", [128, 8, CH], BF16)
        ofm = sbr("ofm", [128, 8, TB], BF16)
        flq = sbr("flq", [128, 7, TB], F32)
        tokv = sbr("tokv", [128, 6, 8], F32)
        dscr = sbr("dscr", [128, 512], BF16)
        datile = sbr("datile", [128, 512], BF16)
        bh_sb = sbr("bh_sb", [128, 512], BF16)
        b_sb2 = sbr("b_sb2", [128, 512], BF16)
        bh_nx = sbr("bh_nx", [128, 512], BF16)
        b_nx = sbr("b_nx", [128, 512], BF16)
        qhat = sbr("qhat", [128, 512], BF16)
        vj = [sbr("vj0", [128, 512], BF16), sbr("vj1", [128, 512], BF16)]
        wsl = sbr("wsl", [128, 512], BF16)
        wps = sbr("wps", [128, 512], BF16)
        t2f = wps
        t1f = sbr("t1f", [128, 512], BF16)
        rsc = sbr("rsc", [65, 512], BF16)
        qraw = sbr("qraw", [128, 8, TB + 3], BF16)
        kraw = sbr("kraw", [128, 8, TB + 3], BF16)
        vraw = sbr("vraw", [128, 8, TB + 3], BF16)
        ov.close()
        ovb = contextlib.ExitStack()
        wg_sb = ovb.enter_context(nc.sbuf_tensor("wg_sb", [128, 8, HID], BF16, side="right"))
        wu_sb = ovb.enter_context(nc.sbuf_tensor("wu_sb", [128, 8, HID], BF16, side="right"))
        ovb.close()
        rem = nc.sbuf_bytes_remaining
        pb_gkk = ps("pb_gkk", [128, 512], F32)
        pb_gkq = ps("pb_gkq", [128, 512], F32)
        pb_exp = ps("pb_exp", [128, 512], F32)
        pb_mm = ps("pb_mm", [128, 512], F32)
        pb_p0 = ps("pb_p0", [128, 512], F32)
        pb_p1 = ps("pb_p1", [128, 512], F32)
        ovp = contextlib.ExitStack()
        pb_o = ovp.enter_context(nc.psum_tensor("pb_o", [128, 512], F32))
        pt_tr = ovp.enter_context(nc.psum_tensor("pt_tr", [128, 1024], BF16))
        ovp.close()
        pb_q0 = ps("pb_q0", [128, 512], F32)
        pb_q1 = ps("pb_q1", [128, 512], F32)

        sems = {k: es.enter_context(nc.semaphore("sem_" + k)) for k in "tvsg"}
        for s in sems.values():
            nc.gpsimd.sem_clear(s)
        nc.all_engine_barrier()
        block = es.enter_context(nc.Block())
        sc = Sched()

        def f_cl(eh, dma):
            dma(ident[:], c_ident_h)
            dma(identf32[:], c_ident_h[0:8, 0:8])
            dma(maskbh[:], c_maskbh_h)
            dma(maskb[:], c_maskb_h)
            dma(maska[:], c_maska_h)
            dma(onesb[:], c_ones_h[:, 0:128])
            dma(onesf32[:], c_ones_h[:, 0:64])
            dma(negonesf32[:], c_ones_h[:, 64:128])
            dma(cvec[:], c_cvec_h)
            dma(qn2[:], qn2_h)
            dma(kn2[:], kn2_h)
            dma(onw8[:], onw8_h)
            dma(lam[:], gs8_h)
            dma(convw[:], convw_h.rearrange("(c p) k -> p c k", p=128))
            dma(bfl[:], bfl_h)
        ev_cl = sc.gop(f_cl, 15, (), "consts")
        ev_neg = sc.add("v", lambda eh: eh.tensor_scalar_mul(negonesf32[:], negonesf32[:], -1.0),
                        [ev_cl], "neg")
        ev_lam = sc.add("s", lambda eh: eh.activation(lam[:], lam[:], AF.Abs), [ev_cl], "lam")

        cast_jobs = []
        for wi, w_h in enumerate((wq_h, wk_h, wv_h, wo_h)):
            for rb in range(8):
                for cb in range(4):
                    cast_jobs.append((wbf_h[wi * D + rb * 128:wi * D + (rb + 1) * 128,
                                            cb * 256:(cb + 1) * 256],
                                      w_h[rb * 128:(rb + 1) * 128, cb * 256:(cb + 1) * 256]))
        for rb in range(8):
            for cb in range(8):
                cast_jobs.append((wrbf_h[rb * 128:(rb + 1) * 128, cb * 256:(cb + 1) * 256],
                                  wr2_h[rb * 128:(rb + 1) * 128, cb * 256:(cb + 1) * 256]))
        ev_half = [None, None]
        for j, (dstp, srcp) in enumerate(cast_jobs):
            hb = j % 2
            stg = dscr[:, hb * 256:(hb + 1) * 256]
            ev_c = sc.gop(lambda eh, dma, stg=stg, srcp=srcp: dma(stg, srcp),
                          1, [ev_half[hb]], f"wc{j}")
            ev_half[hb] = sc.gop(lambda eh, dma, dstp=dstp, stg=stg: dma(dstp, stg),
                                 1, [ev_c], f"ws{j}")
        ev_c = sc.gop(lambda eh, dma: dma(dscr[:, 0:256].rearrange("p (c f) -> p c f", f=32),
                                          wfl_h.rearrange("(c p) f -> p c f", p=128)),
                      1, [ev_half[0]], "wcfl")
        ev_half[0] = sc.gop(lambda eh, dma: dma(wflbf_h.rearrange("(c p) f -> p c f", p=128),
                                                dscr[:, 0:256].rearrange("p (c f) -> p c f", f=32)),
                            1, [ev_c], "wsfl")
        ev_cast = [ev_half[0], ev_half[1]]

        prev_b_done_v = None
        prev_b_done_t = None
        prev_seq_fst = None
        wr_prev_use = [None, None, None]
        ev_wom = None
        ev_ost = None
        ev_dmms = None

        for s_i in range(n_seq):
            def f_wl(eh, dma):
                for wi, w_sb_ in enumerate((wq_sb, wk_sb, wv_sb, wo_sb)):
                    for c in range(8):
                        dma(w_sb_[:, c, :], wbf_h[wi * D + c * 128:wi * D + (c + 1) * 128, :])
                dma(wfl_sb[:], wflbf_h.rearrange("(c p) f -> p c f", p=128))
            if stage >= 2:
                ev_wl = sc.gop(f_wl, 33,
                               [w for w in (prev_b_done_v, prev_b_done_t, ev_cl) if w] + ev_cast,
                               f"wl{s_i}")
            else:
                ev_wl = ev_cl

            def f_wofold(eh):
                last = None
                for c in range(8):
                    last = eh.tensor_scalar_mul(wo_sb[:, c, :], wo_sb[:, c, :], onw8[:, c:c + 1])
                return last
            ev_wofold = (sc.add("v", f_wofold, [ev_wl, ev_cl, ev_neg], f"wof{s_i}")
                         if stage >= 2 else ev_neg)

            def f_rst(eh):
                eh.memset(S_sb[:], 0.0)
                eh.memset(S_bf[:], 0.0)
                eh.memset(flq[:], 0.0)
                eh.memset(tokv[:], 0.0)
                eh.memset(qraw[:, :, 0:3], 0.0)
                eh.memset(kraw[:, :, 0:3], 0.0)
                return eh.memset(vraw[:, :, 0:3], 0.0)
            ev_rst = sc.add("v", f_rst, [prev_b_done_v] if prev_b_done_v else [], f"rst{s_i}")

            xfree_hist = {0: prev_b_done_t, 1: prev_b_done_t}
            ev_ost = prev_seq_fst
            ev_rawfree = None
            ev_Supd = ev_rst
            ev_xnw_last = None
            prev_kte = prev_b_done_t
            prev_omm = None
            prev_oev = None
            prev_qh = None
            prev_qe = None
            prev_smm = None
            prev_smm_box = [None]
            prev_sqr0 = None
            prev_sqe4 = None
            prev_oadd = None
            prev_ap4 = None
            prev_xt_guard = None
            prev_xh_guard = prev_b_done_t
            prev_xl_guard = prev_b_done_t
            prev_xlo_st = None
            stage_used = {0: None, 1: None}
            gci = 0

            for b in range(n_blk):
                t0 = b * TB
                xr0 = s_i * SEQ + t0
                dbg_here = (s_i == 0 and b == 0)
                xb = b % 2
                x_slab = x_slab2[:, xb]

                ev_xl = sc.gop(lambda eh, dma, xr0=xr0, x_slab=x_slab:
                               dma(x_slab[:], x_h[xr0:xr0 + 128, :]),
                               1, [xfree_hist[xb]], f"xl{s_i}_{b}")
                ev_sq = sc.add("s", lambda eh, x_slab=x_slab:
                               eh.activation(xt_slab[:], x_slab[:], AF.Square),
                               [ev_xl, prev_xt_guard], f"lnsq{s_i}_{b}")
                def f_lred(eh, x_slab=x_slab):
                    eh.tensor_reduce(lnv[:, 0:1], x_slab[:], AX.X, OP.add)
                    eh.tensor_reduce(lnv[:, 1:2], xt_slab[:], AX.X, OP.add)
                    eh.drain()
                    eh.tensor_scalar_mul(lnv[:, 2:3], lnv[:, 0:1], 1.0 / D)
                    eh.tensor_scalar_mul(lnv[:, 1:2], lnv[:, 1:2], 1.0 / D)
                    eh.tensor_tensor(lnv[:, 3:4], lnv[:, 2:3], lnv[:, 2:3], OP.mult)
                    eh.drain()
                    return eh.tensor_sub(lnv[:, 1:2], lnv[:, 1:2], lnv[:, 3:4])
                ev_lred = sc.add("v", f_lred, [ev_sq], f"lred{s_i}_{b}")
                def f_lsq(eh):
                    eh.activation(lnv[:, 4:5], lnv[:, 1:2], AF.Ln, bias=cvec[:, 0:1])
                    return eh.activation(lnv[:, 3:4], lnv[:, 4:5], AF.Exp, scale=-0.5)
                ev_lsq = sc.add("s", f_lsq, [ev_lred], f"lsq{s_i}_{b}")
                def f_xn(eh, x_slab=x_slab):
                    return eh.tensor_scalar(xt_slab[:], x_slab[:], lnv[:, 2:3], lnv[:, 3:4],
                                            OP.subtract, OP.mult)
                ev_xn = sc.add("v", f_xn, [ev_lsq], f"xn{s_i}_{b}")
                ev_hilo = sc.add("s", lambda eh: eh.activation(xh_slab[:], xt_slab[:], AF.Copy),
                                 [ev_xn, prev_xh_guard], f"hilo{s_i}_{b}")
                ev_lo = sc.add("v", lambda eh: eh.tensor_sub(xl_slab[:], xt_slab[:], xh_slab[:]),
                               [ev_hilo, prev_xl_guard], f"lo{s_i}_{b}")
                def f_xtr(eh):
                    last = None
                    for c in range(8):
                        last = eh.transpose(pt_tr[:, c * 128:(c + 1) * 128],
                                            xh_slab[:, c * 128:(c + 1) * 128], ident[:])
                    return last
                ev_xtr = sc.add("t", f_xtr, [ev_hilo, ev_cl, prev_kte], f"xtr{s_i}_{b}")
                def f_xnw(eh, t0=t0):
                    return eh.activation(xn_seq[:, :, t0:t0 + TB],
                                         pt_tr[:].rearrange("p (c t) -> p c t", c=8), AF.Copy)
                ev_xnw = sc.add("s", f_xnw, [ev_xtr], f"xnw{s_i}_{b}")
                def f_xtr2(eh):
                    last = None
                    for c in range(8):
                        last = eh.transpose(pt_tr[:, c * 128:(c + 1) * 128],
                                            xl_slab[:, c * 128:(c + 1) * 128], ident[:])
                    return last
                ev_xtr2 = sc.add("t", f_xtr2, [ev_lo, ev_xnw], f"xtr2{s_i}_{b}")
                def f_xnw2(eh):
                    return eh.activation(xlo_stage[:],
                                         pt_tr[:].rearrange("p (c t) -> p c t", c=8), AF.Copy)
                ev_xnw2 = sc.add("s", f_xnw2, [ev_xtr2, prev_xlo_st], f"xnw2{s_i}_{b}")
                ev_xlo_st = sc.gop(lambda eh, dma, t0=t0:
                                   dma(xlo_h[:, :, t0:t0 + TB], xlo_stage[:]),
                                   1, [ev_xnw2], f"xlst{s_i}_{b}")
                prev_xlo_st = ev_xlo_st
                prev_xh_guard = ev_xtr
                prev_xl_guard = ev_xtr2
                ev_xnw_last = ev_xnw2
                if dbg_here and "xn" in dbg:
                    sc.gop(lambda eh, dma: dma(dbg_h["xn"].rearrange("p (c t) -> p c t", t=TB),
                                               xn_seq[:, :, 0:TB]),
                           1, [ev_xnw], "dbgxn")

                if dbg_here and "lnv" in dbg:
                    sc.gop(lambda eh, dma: dma(dbg_h["lnv"][:], lnv[:]), 1, [ev_xn],
                           "dbglnv")
                if dbg_here and "xt" in dbg:
                    sc.gop(lambda eh, dma: dma(dbg_h["xt"][:], xt_slab[:]), 1, [ev_xn],
                           "dbgxt")
                if stage < 2:
                    xfree_hist[xb] = ev_xn
                    prev_xt_guard = ev_lo
                    continue
                if b > 0:
                    def f_hist(eh):
                        eh.tensor_copy(qraw[:, :, 0:3], qraw[:, :, TB:TB + 3])
                        eh.tensor_copy(kraw[:, :, 0:3], kraw[:, :, TB:TB + 3])
                        return eh.tensor_copy(vraw[:, :, 0:3], vraw[:, :, TB:TB + 3])
                    ev_hist = sc.add("v", f_hist, [ev_rawfree], f"hist{s_i}_{b}")
                else:
                    ev_hist = ev_rst

                ev_conv = {}
                prev_evac = {}
                for mi, (w_sb_, raw, fmt) in enumerate(((wq_sb, qraw, qfm), (wk_sb, kraw, kfm),
                                                        (wv_sb, vraw, vfm))):
                    for m in range(8):
                        idx = mi * 8 + m
                        pbank = pb_p0 if idx % 2 == 0 else pb_p1
                        def f_pmm(eh, w_sb_=w_sb_, m=m, t0=t0, pbank=pbank):
                            last = None
                            for d in range(8):
                                last = eh.matmul(pbank[:, 0:TB],
                                                 w_sb_[:, d, m * 128:(m + 1) * 128],
                                                 xn_seq[:, d, t0:t0 + TB],
                                                 start=(d == 0), stop=(d == 7))
                            return last
                        waits = [ev_xnw, ev_wl]
                        if idx >= 2:
                            waits.append(prev_evac[idx - 2])
                        else:
                            waits.append(prev_oadd)
                        ev_pm = sc.add("t", f_pmm, waits, f"pj{s_i}_{b}_{mi}_{m}")
                        def f_pev(eh, raw=raw, m=m, pbank=pbank):
                            return eh.activation(raw[:, m, 3:3 + TB], pbank[:, 0:TB], AF.Copy)
                        ev_pe = sc.add("s", f_pev, [ev_pm, ev_hist], f"pe{s_i}_{b}_{mi}_{m}")
                        prev_evac[idx] = ev_pe
                        def f_cv(eh, raw=raw, fmt=fmt, m=m, mi=mi):
                            w = convw[:, m, mi * 4:mi * 4 + 4]
                            eh.tensor_scalar_mul(t1f[:, 0:TB], raw[:, m, 0:TB], w[:, 0:1])
                            eh.scalar_tensor_tensor(t1f[:, 0:TB], raw[:, m, 1:1 + TB], w[:, 1:2],
                                                    t1f[:, 0:TB], OP.mult, OP.add)
                            eh.scalar_tensor_tensor(t1f[:, 0:TB], raw[:, m, 2:2 + TB], w[:, 2:3],
                                                    t1f[:, 0:TB], OP.mult, OP.add)
                            return eh.scalar_tensor_tensor(fmt[:, m, :], raw[:, m, 3:3 + TB],
                                                           w[:, 3:4], t1f[:, 0:TB],
                                                           OP.mult, OP.add)
                        ev_conv[(mi, m)] = sc.add("v", f_cv, [ev_pe, prev_omm],
                                                  f"cv{s_i}_{b}_{mi}_{m}")
                ev_rawfree = ev_conv[(2, 7)]
                if dbg_here:
                    for nm, tile in (("qfm", qfm), ("kfm", kfm), ("vfm", vfm)):
                        if nm in dbg:
                            sc.gop(lambda eh, dma, nm=nm, tile=tile:
                                   dma(dbg_h[nm][:], tile[:].rearrange("p c t -> p (c t)")),
                                   1, [ev_conv[(2, 7)]], "dbg" + nm)
                if stage < 3:
                    xfree_hist[xb] = ev_xn
                    prev_xt_guard = ev_lo
                    continue

                def f_flmm(eh, t0=t0):
                    last = None
                    for li in range(2):
                        colbase = li * TB
                        for half in range(2):
                            wcol = li * 16 + half * 8
                            part0 = 0 if half == 0 else 64
                            pos = (0, 0) if half == 0 else (0, 64)
                            for d in range(8):
                                last = eh.matmul(pb_exp[part0:part0 + 8, colbase:colbase + TB],
                                                 wfl_sb[:, d, wcol:wcol + 8],
                                                 xn_seq[:, d, t0:t0 + TB],
                                                 start=(d == 0), stop=False, tile_position=pos)
                            last = eh.matmul(pb_exp[part0:part0 + 8, colbase:colbase + TB],
                                             bfl[0:1, wcol:wcol + 8], onesb[0:1, 0:TB],
                                             start=False, stop=True, tile_position=pos)
                    return last
                ev_fl = sc.add("t", f_flmm, [ev_xnw, ev_wl, ev_cl, prev_qe], f"fl{s_i}_{b}")
                def f_flev(eh):
                    eh.activation(flq[0:8, 0, :], pb_exp[0:8, 0:TB], AF.Copy)
                    eh.activation(flq[64:72, 0, :], pb_exp[64:72, 0:TB], AF.Copy)
                    eh.activation(flq[0:8, 1, :], pb_exp[0:8, TB:2 * TB], AF.Copy)
                    return eh.activation(flq[64:72, 1, :], pb_exp[64:72, TB:2 * TB], AF.Copy)
                ev_flev = sc.add("s", f_flev, [ev_fl], f"flev{s_i}_{b}")
                def f_ls1(eh):
                    eh.activation(flq[:, 2, :], flq[:, 0, :], AF.Abs)
                    eh.activation(flq[:, 2, :], flq[:, 2, :], AF.Exp, scale=-1.0)
                    eh.activation(flq[:, 2, :], flq[:, 2, :], AF.Ln, bias=cvec[:, 3:4])
                    eh.activation(flq[:, 3, :], flq[:, 1, :], AF.Abs)
                    eh.activation(flq[:, 3, :], flq[:, 3, :], AF.Exp, scale=-1.0)
                    eh.activation(flq[:, 3, :], flq[:, 3, :], AF.Ln, bias=cvec[:, 3:4])
                    return eh.activation(flq[:, 6, :], flq[:, 1, :], AF.Sigmoid)
                ev_ls1 = sc.add("s", f_ls1, [ev_flev], f"ls1{s_i}_{b}")
                def f_ls2(eh):
                    eh.tensor_scalar_min(flq[:, 4, :], flq[:, 0, :], 0.0)
                    eh.tensor_sub(flq[:, 2, :], flq[:, 4, :], flq[:, 2, :])
                    eh.tensor_scalar_min(flq[:, 4, :], flq[:, 1, :], 0.0)
                    return eh.tensor_sub(flq[:, 3, :], flq[:, 4, :], flq[:, 3, :])
                ev_ls2 = sc.add("v", f_ls2, [ev_ls1], f"ls2{s_i}_{b}")
                def f_g(eh):
                    for c in range(NCHB):
                        sl = slice(c * CH, (c + 1) * CH)
                        eh.tensor_tensor_scan(flq[:, 4, sl], onesf32[:, 0:CH], flq[:, 2, sl],
                                              0.0, OP.mult, OP.add)
                    eh.drain()
                    eh.tensor_sub(flq[:, 0, :], flq[:, 4, :], flq[:, 2, :])
                    return eh.tensor_sub(flq[:, 5, :], flq[:, 3, :], flq[:, 4, :])
                ev_g = sc.add("v", f_g, [ev_ls2], f"g{s_i}_{b}")
                if dbg_here and "flq" in dbg:
                    sc.gop(lambda eh, dma: dma(dbg_h["flq"][:],
                                               flq[:].rearrange("p c t -> p (c t)")),
                           1, [ev_g], "dbgflq")

                def emit_tilde(which, ev_deps):
                    mi_src = 0 if which == "q" else 1
                    fm_t = qfm if which == "q" else kfm
                    wcolbase = 0 if which == "q" else D
                    def fm_fat(tile, hsl):
                        return tile[:, :, hsl]
                    def r3(flat):
                        return flat.rearrange("p (c t) -> p c t", t=CH)
                    evs_re = []
                    for m in range(8):
                        pbank = pb_p0 if m % 2 == 0 else pb_p1
                        mb = m % 3
                        def f_wrl(eh, dma, m=m, mb=mb, wcolbase=wcolbase):
                            colb = wcolbase + m * 128
                            dma(wrbuf[:, mb], wrbf_h[:, colb:colb + 128]
                                .rearrange("(c p) w -> p c w", p=128))
                        ev_wrl = sc.gop(f_wrl, 1, [wr_prev_use[mb], ev_wl],
                                        f"wrl{which}{s_i}_{b}_{m}")
                        def f_rmm(eh, t0=t0, pbank=pbank, mb=mb):
                            last = None
                            for d in range(8):
                                last = eh.matmul(
                                    pbank[:, TB:2 * TB],
                                    wrbuf[:, mb, d, :],
                                    xn_seq[:, d, t0:t0 + TB],
                                    start=(d == 0), stop=(d == 7))
                            return last
                        waits = [ev_xnw, ev_wrl] + (list(ev_deps) if m < 2 else [])
                        if m >= 2:
                            waits.append(evs_re[m - 2])
                        if m < 2:
                            waits.append(prev_evac[22 + (m % 2)])
                        ev_rm = sc.add("t", f_rmm, waits, f"r{which}{s_i}_{b}_{m}")
                        wr_prev_use[mb] = ev_rm
                        def f_rev(eh, m=m, pbank=pbank):
                            return eh.activation(rfm[:, m, :], pbank[:, TB:2 * TB], AF.Copy)
                        ev_re = sc.add("s", f_rev,
                                       [ev_rm] + (list(ev_deps) if m < 2 else []),
                                       f"re{which}{s_i}_{b}_{m}")
                        evs_re.append(ev_re)
                    ev_out = None
                    for hb in range(2):
                        hsl = slice(hb * CH, (hb + 1) * CH)
                        sq_waits = [ev_conv[(mi_src, 7)], ev_xnw, ev_out]
                        if hb == 0:
                            sq_waits += list(ev_deps)
                        ev_sq2 = sc.add("s", lambda eh, fm_t=fm_t, hsl=hsl:
                                        eh.activation(r3(sqscr[:, 0:512]), fm_fat(fm_t, hsl),
                                                      AF.Square),
                                        sq_waits, f"sq{which}{s_i}_{b}_{hb}")
                        def f_cs(eh):
                            last = None
                            for p in range(8):
                                last = eh.matmul(pb_exp[0:1, p * CH:(p + 1) * CH],
                                                 onesb[0:64, 0:1],
                                                 sqscr[0:64, p * CH:(p + 1) * CH],
                                                 start=True, stop=True)
                                last = eh.matmul(pb_exp[64:65, p * CH:(p + 1) * CH],
                                                 onesb[64:128, 0:1],
                                                 sqscr[64:128, p * CH:(p + 1) * CH],
                                                 start=True, stop=True, tile_position=(64, 64))
                            return last
                        ev_cs = sc.add("t", f_cs, [ev_sq2, ev_flev], f"cs{which}{s_i}_{b}_{hb}")
                        def f_rs(eh):
                            eh.activation(rsc[0:1, 0:512], pb_exp[0:1, 0:512], AF.Sqrt,
                                          bias=cvec[0:1, 1:2], scale=1.0 / DH)
                            return eh.activation(rsc[64:65, 0:512], pb_exp[64:65, 0:512],
                                                 AF.Sqrt, bias=cvec[64:65, 1:2],
                                                 scale=1.0 / DH)
                        ev_rs = sc.add("s", f_rs, [ev_cs], f"rs{which}{s_i}_{b}_{hb}")
                        def f_rr(eh):
                            with nc.allow_low_precision(reason="bf16 rstd feeds bf16 matmul"):
                                eh.reciprocal(rsc[0:1, 0:512], rsc[0:1, 0:512])
                                return eh.reciprocal(rsc[64:65, 0:512], rsc[64:65, 0:512])
                        ev_rr = sc.add("v", f_rr, [ev_rs], f"rr{which}{s_i}_{b}_{hb}")
                        def f_bc(eh):
                            eh.matmul(pb_exp[0:64, 0:512], onesb[0:1, 0:64], rsc[0:1, 0:512],
                                      start=True, stop=True)
                            return eh.matmul(pb_exp[64:128, 0:512], onesb[64:65, 0:64],
                                             rsc[64:65, 0:512], start=True, stop=True,
                                             tile_position=(64, 64))
                        ev_bc = sc.add("t", f_bc, [ev_rr], f"bc{which}{s_i}_{b}_{hb}")
                        def f_lr(eh, hsl=hsl):
                            last = None
                            for p in range(8):
                                last = eh.tensor_scalar_mul(t2f[:, p * CH:(p + 1) * CH],
                                                            rfm[:, p, hsl], lam[:, p:p + 1])
                            return last
                        ev_lr = sc.add("v", f_lr, [evs_re[7], ev_lam, prev_smm_box[0]], f"lr{which}{s_i}_{b}_{hb}")
                        nw = qn2 if which == "q" else kn2
                        def f_cb(eh, fm_t=fm_t, hsl=hsl, nw=nw):
                            eh.tensor_tensor(r3(t1f[:, 0:512]), fm_fat(fm_t, hsl),
                                             r3(pb_exp[:, 0:512]), OP.mult)
                            return eh.scalar_tensor_tensor(fm_fat(fm_t, hsl), r3(t1f[:, 0:512]),
                                                           nw[:, 0:1], r3(t2f[:, 0:512]),
                                                           OP.mult, OP.add)
                        ev_cb = sc.add("v", f_cb, [ev_bc, ev_lr, prev_omm],
                                       f"cb{which}{s_i}_{b}_{hb}")
                        if which == "k":
                            ev_sq3 = sc.add("s", lambda eh, hsl=hsl:
                                            eh.activation(r3(sqscr[:, 0:512]), fm_fat(kfm, hsl),
                                                          AF.Square),
                                            [ev_cb], f"ksq{s_i}_{b}_{hb}")
                            ev_cs2 = sc.add("t", f_cs, [ev_sq3], f"kcs{s_i}_{b}_{hb}")
                            def f_rs2(eh):
                                eh.activation(rsc[0:1, 0:512], pb_exp[0:1, 0:512], AF.Sqrt)
                                return eh.activation(rsc[64:65, 0:512], pb_exp[64:65, 0:512],
                                                     AF.Sqrt)
                            ev_rs2 = sc.add("s", f_rs2, [ev_cs2], f"krs{s_i}_{b}_{hb}")
                            def f_rr2(eh):
                                eh.tensor_scalar_max(rsc[0:1, 0:512], rsc[0:1, 0:512], 1e-12)
                                eh.tensor_scalar_max(rsc[64:65, 0:512], rsc[64:65, 0:512], 1e-12)
                                with nc.allow_low_precision(reason="bf16 rstd feeds bf16 matmul"):
                                    eh.reciprocal(rsc[0:1, 0:512], rsc[0:1, 0:512])
                                    return eh.reciprocal(rsc[64:65, 0:512], rsc[64:65, 0:512])
                            ev_rr2 = sc.add("v", f_rr2, [ev_rs2], f"krr{s_i}_{b}_{hb}")
                            ev_bc2 = sc.add("t", f_bc, [ev_rr2], f"kbc{s_i}_{b}_{hb}")
                            ev_cb = sc.add("v", lambda eh, hsl=hsl:
                                           eh.tensor_tensor(fm_fat(kfm, hsl), fm_fat(kfm, hsl),
                                                            r3(pb_exp[:, 0:512]), OP.mult),
                                           [ev_bc2], f"knm{s_i}_{b}_{hb}")
                        ev_out = ev_cb
                    return ev_out

                ev_qt = emit_tilde("q", [ev_wom] if ev_wom else [])
                ev_kt = emit_tilde("k", [ev_qt])
                if dbg_here:
                    for nm, tile, evd in (("qtil", qfm, ev_qt), ("ktil", kfm, ev_kt)):
                        if nm in dbg:
                            sc.gop(lambda eh, dma, nm=nm, tile=tile:
                                   dma(dbg_h[nm][:], tile[:].rearrange("p c t -> p (c t)")),
                                   1, [evd], "dbg" + nm)
                if stage < 4:
                    xfree_hist[xb] = ev_xn
                    prev_xt_guard = ev_lo
                    continue

                ev_oev = prev_oev
                for c in range(NCHB):
                    csl = slice(c * CH, (c + 1) * CH)
                    dbg_ch = dbg_here and c == 0
                    cp = gci % 2
                    gci += 1
                    so = cp * 3104
                    po = cp * 256
                    go = cp * 4 * CH

                    def f_stg1(eh, dma, csl=csl, go=go):
                        dma(stg_dram[0:8, go + 0:go + CH], flq[0:8, 4, csl])
                        dma(stg_dram[8:16, go + 0:go + CH], flq[64:72, 4, csl])
                        dma(stg_dram[0:8, go + CH:go + 2 * CH], flq[0:8, 0, csl])
                        dma(stg_dram[8:16, go + CH:go + 2 * CH], flq[64:72, 0, csl])
                        dma(stg_dram[0:8, go + 2 * CH:go + 3 * CH], flq[0:8, 5, csl])
                        dma(stg_dram[8:16, go + 2 * CH:go + 3 * CH], flq[64:72, 5, csl])
                        dma(stg_dram[0:8, go + 3 * CH:go + 4 * CH], flq[0:8, 6, csl])
                        dma(stg_dram[8:16, go + 3 * CH:go + 4 * CH], flq[64:72, 6, csl])
                    ev_st1 = sc.gop(f_stg1, 8, [ev_g, stage_used[cp]], f"st1{s_i}_{b}_{c}")
                    def f_stg2(eh, dma, so=so, po=po, go=go):
                        for q in range(3):
                            dma(stage_sb[0:1, so + q * 1024:so + (q + 1) * 1024]
                                .rearrange("o (r t) -> o r t", t=CH),
                                stg_dram[:, go + q * CH:go + (q + 1) * CH]
                                .rearrange("(o r) t -> o r t", o=1))
                        for j, q in enumerate((0, 3)):
                            dma(stage_pv[0:8, po + j * 128:po + (j + 1) * 128]
                                .rearrange("p (hf t) -> p hf t", hf=2),
                                stg_dram[:, go + q * CH:go + (q + 1) * CH]
                                .rearrange("(hf p) t -> p hf t", hf=2))
                    ev_st2 = sc.gop(f_stg2, 5, [ev_st1], f"st2{s_i}_{b}_{c}")
                    if dbg_ch and "stagerow" in dbg:
                        sc.gop(lambda eh, dma: dma(dbg_h["stagerow"][:], stage_sb[:, 0:3104]),
                               1, [ev_st2], "dbgst")

                    def f_tv(eh, po=po):
                        eh.matmul(pb_mm[:, 0:8], stage_pv[0:8, po:po + 128], identf32[0:8, 0:8],
                                  start=True, stop=True, skip_group_check=True)
                        return eh.matmul(pb_mm[:, 8:16], stage_pv[0:8, po + 128:po + 256],
                                         identf32[0:8, 0:8], start=True, stop=True,
                                         skip_group_check=True)
                    if chsub < 2:
                        stage_used[cp] = ev_st2
                        continue
                    ev_tv = sc.add("t", f_tv, [ev_st2, ev_g, ev_kt, ev_Supd], f"tv{s_i}_{b}_{c}")
                    def f_tve(eh):
                        eh.activation(tokv[:, 0, :], pb_mm[:, 0:8], AF.Copy)
                        return eh.activation(tokv[:, 1, :], pb_mm[:, 8:16], AF.Copy)
                    ev_tve = sc.add("s", f_tve, [ev_tv], f"tve{s_i}_{b}_{c}")
                    def f_gc3(eh, so=so):
                        eh.matmul(pb_mm[0:64, 16:24], onesf32[0:1, 0:64],
                                  stage_sb[0:1, so + 63:so + 512:64], start=True, stop=True,
                                  skip_group_check=True)
                        return eh.matmul(pb_mm[64:128, 16:24], onesf32[0:1, 0:64],
                                         stage_sb[0:1, so + 575:so + 1024:64], start=True,
                                         stop=True, tile_position=(0, 64),
                                         skip_group_check=True)
                    ev_gc3 = sc.add("t", f_gc3, [ev_st2, ev_tve], f"gc3{s_i}_{b}_{c}")
                    def f_gc4(eh):
                        eh.tensor_copy(tokv[:, 2, :], pb_mm[:, 16:24])
                        eh.drain()
                        return eh.tensor_sub(tokv[:, 4, :], tokv[:, 2, :], tokv[:, 0, :])
                    ev_gc4 = sc.add("v", f_gc4, [ev_gc3], f"gc4{s_i}_{b}_{c}")
                    def f_gc5(eh):
                        eh.activation(tokv[:, 4, :], tokv[:, 4, :], AF.Exp)
                        return eh.activation(tokv[:, 3, :], tokv[:, 2, :], AF.Exp)
                    ev_gc5 = sc.add("s", f_gc5, [ev_gc4], f"gc5{s_i}_{b}_{c}")
                    if dbg_ch and "gtok" in dbg:
                        sc.gop(lambda eh, dma: dma(dbg_h["gtok"][:],
                                                   tokv[:].rearrange("p c t -> p (c t)")),
                               1, [ev_gc5], "dbggtok")

                    def f_ktr(eh, csl=csl):
                        last = None
                        for p in range(8):
                            last = eh.transpose(pt_tr[0:64, p * CH:(p + 1) * CH],
                                                kfm[0:64, p, csl], ident[0:64, 0:64])
                            last = eh.transpose(pt_tr[64:128, p * CH:(p + 1) * CH],
                                                kfm[64:128, p, csl], ident[64:128, 64:128],
                                                tile_position=(64, 64))
                            last = eh.transpose(pt_tr[0:64, 512 + p * CH:512 + (p + 1) * CH],
                                                vfm[0:64, p, csl], ident[0:64, 0:64])
                            last = eh.transpose(pt_tr[64:128, 512 + p * CH:512 + (p + 1) * CH],
                                                vfm[64:128, p, csl], ident[64:128, 64:128],
                                                tile_position=(64, 64))
                        return last
                    ev_ktr = sc.add("t", f_ktr, [ev_kt, ev_conv[(2, 7)], ev_xnw2, prev_kte],
                                    f"ktr{s_i}_{b}_{c}")
                    def f_kte(eh):
                        eh.activation(ktok[:].rearrange("p c t -> p (c t)"), pt_tr[:, 0:512],
                                      AF.Copy)
                        return eh.activation(vtok[:].rearrange("p c t -> p (c t)"),
                                             pt_tr[:, 512:1024], AF.Copy)
                    ev_kte = sc.add("s", f_kte, [ev_ktr, prev_smm], f"kte{s_i}_{b}_{c}")
                    prev_kte = ev_kte

                    if chsub < 3:
                        stage_used[cp] = ev_gc3
                        continue
                    def f_gram(eh, csl=csl):
                        last = None
                        for p in range(8):
                            cc = slice(p * CH, (p + 1) * CH)
                            last = eh.matmul(pb_gkk[0:64, cc], kfm[0:64, p, csl],
                                             kfm[0:64, p, csl], start=True, stop=True,
                                             tile_position=(0, 0))
                            last = eh.matmul(pb_gkk[64:128, cc], kfm[64:128, p, csl],
                                             kfm[64:128, p, csl], start=True, stop=True,
                                             tile_position=(64, 64))
                            last = eh.matmul(pb_gkq[0:64, cc], kfm[0:64, p, csl],
                                             qfm[0:64, p, csl], start=True, stop=True,
                                             tile_position=(0, 0))
                            last = eh.matmul(pb_gkq[64:128, cc], kfm[64:128, p, csl],
                                             qfm[64:128, p, csl], start=True, stop=True,
                                             tile_position=(64, 64))
                        return last
                    ev_gram = sc.add("t", f_gram, [ev_kt, ev_qt, prev_sqe4],
                                     f"gram{s_i}_{b}_{c}")

                    def decay(dm_waits, dt_waits, mask_t, qrow, qpp, ppneg, outtile, grambank,
                              tag):
                        def f_dm(eh, mask_t=mask_t, qrow=qrow, qpp=qpp, ppneg=ppneg, so=so):
                            eh.matmul(pb_exp[:, 0:512], ident[:], mask_t[:, 0:512],
                                      start=True, stop=False, skip_group_check=True)
                            eh.matmul(pb_exp[0:64, 0:512], onesf32[0:1, 0:64],
                                      stage_sb[0:1, so + qrow * 1024:so + qrow * 1024 + 512],
                                      start=False, stop=False, skip_group_check=True)
                            eh.matmul(pb_exp[64:128, 0:512], onesf32[0:1, 0:64],
                                      stage_sb[0:1, so + qrow * 1024 + 512:so + qrow * 1024 + 1024],
                                      start=False, stop=False, tile_position=(0, 64),
                                      skip_group_check=True)
                            last = None
                            rhs1 = negonesf32 if ppneg else onesf32
                            for p in range(8):
                                base = so + qpp * 1024
                                last = eh.matmul(
                                    pb_exp[0:64, p * CH:(p + 1) * CH],
                                    stage_sb[0:1, base + p * CH:base + (p + 1) * CH],
                                    rhs1[0:1, 0:CH], start=False, stop=True,
                                    skip_group_check=True)
                                last = eh.matmul(
                                    pb_exp[64:128, p * CH:(p + 1) * CH],
                                    stage_sb[0:1, base + 512 + p * CH:base + 512 + (p + 1) * CH],
                                    rhs1[0:1, 0:CH], start=False, stop=True,
                                    tile_position=(0, 64), skip_group_check=True)
                            return last
                        ev_dm = sc.add("t", f_dm, dm_waits, f"dm{tag}")
                        ev_de = sc.add("s", lambda eh: eh.activation(dscr[:, 0:512],
                                                                     pb_exp[:, 0:512], AF.Exp),
                                       [ev_dm], f"de{tag}")
                        def f_dt(eh, outtile=outtile, grambank=grambank):
                            return eh.tensor_tensor(outtile[:, 0:512], dscr[:, 0:512],
                                                    grambank[:, 0:512], OP.mult)
                        ev_dt = sc.add("v", f_dt, [ev_de, ev_gram] + list(dt_waits), f"dt{tag}")
                        return ev_dm, ev_de, ev_dt

                    ev_dm1, ev_de1, ev_bh = decay([ev_gram, ev_st2, ev_gc3, ev_cl, ev_kt, prev_qe],
                                                  [prev_sqr0], maskbh, 1, 2, False, bh_sb,
                                                  pb_gkk, f"bh{s_i}_{b}_{c}")
                    ev_dm2, ev_de2, ev_b2 = decay([ev_bh], [prev_sqr0], maskb, 2, 1, False,
                                                  b_sb2, pb_gkk, f"b{s_i}_{b}_{c}")
                    ev_dm3, ev_de3, ev_da = decay([ev_b2], [prev_omm], maska, 0, 0, True,
                                                  datile, pb_gkq, f"da{s_i}_{b}_{c}")
                    if dbg_ch and "bh" in dbg:
                        sc.gop(lambda eh, dma: dma(dbg_h["bh"][:], bh_sb[:]), 1, [ev_bh], "dbgbh")

                    def f_qm(eh, so=so):
                        eh.matmul(pb_exp[0:64, 0:512], onesf32[0:1, 0:64],
                                  stage_sb[0:1, so + 0:so + 512], start=True, stop=True,
                                  skip_group_check=True)
                        return eh.matmul(pb_exp[64:128, 0:512], onesf32[0:1, 0:64],
                                         stage_sb[0:1, so + 512:so + 1024],
                                         start=True, stop=True, tile_position=(0, 64),
                                         skip_group_check=True)
                    ev_qm = sc.add("t", f_qm, [ev_da], f"qm{s_i}_{b}_{c}")
                    ev_qe = sc.add("s", lambda eh: eh.activation(dscr[:, 0:512], pb_exp[:, 0:512],
                                                                 AF.Exp),
                                   [ev_qm, ev_da, prev_qh], f"qe{s_i}_{b}_{c}")
                    prev_qe = ev_qe
                    def f_qh(eh, csl=csl):
                        return eh.tensor_tensor(qhat[:, 0:512].rearrange("p (c t) -> p c t", t=CH),
                                                dscr[:, 0:512].rearrange("p (c t) -> p c t", t=CH),
                                                qfm[:, :, csl], OP.mult)
                    ev_qh = sc.add("v", f_qh, [ev_qe, ev_qt, prev_omm], f"qh{s_i}_{b}_{c}")
                    prev_qh = ev_qh

                    if chsub < 4:
                        stage_used[cp] = ev_qm
                        continue
                    def f_retr(eh, csl=csl):
                        last = None
                        for p in range(8):
                            cc = slice(p * CH, (p + 1) * CH)
                            last = eh.matmul(pb_mm[0:64, cc], kfm[0:64, p, csl],
                                             S_bf[0:64, p, :], start=True, stop=True,
                                             tile_position=(0, 0))
                            last = eh.matmul(pb_mm[64:128, cc], kfm[64:128, p, csl],
                                             S_bf[64:128, p, :], start=True, stop=True,
                                             tile_position=(64, 64))
                        return last
                    ev_retr = sc.add("t", f_retr, [ev_kt, ev_Supd, ev_tve, ev_gc4],
                                     f"rt{s_i}_{b}_{c}")
                    def f_rhs(eh):
                        return eh.tensor_sub(vj[0][:, 0:512],
                                             vtok[:].rearrange("p c t -> p (c t)"),
                                             pb_mm[:, 0:512])
                    ev_rhs = sc.add("v", f_rhs, [ev_retr, ev_kte], f"rhs{s_i}_{b}_{c}")

                    def f_ap0(eh):
                        last = None
                        for p in range(8):
                            cc = slice(p * CH, (p + 1) * CH)
                            last = eh.matmul(pb_mm[0:64, cc], bh_sb[0:64, cc], vj[0][0:64, cc],
                                             start=True, stop=True, tile_position=(0, 0))
                            last = eh.matmul(pb_mm[64:128, cc], bh_sb[64:128, cc],
                                             vj[0][64:128, cc], start=True, stop=True,
                                             tile_position=(64, 64))
                        return last
                    ev_a0 = sc.add("t", f_ap0, [ev_bh, ev_rhs], f"a0{s_i}_{b}_{c}")
                    ev_vj = sc.add("v", lambda eh: eh.tensor_sub(vj[1][:, 0:512], vj[0][:, 0:512],
                                                                 pb_mm[:, 0:512]),
                                   [ev_a0], f"ae0{s_i}_{b}_{c}")
                    vcur = 1
                    cur_bh, cur_b = bh_sb, b_sb2
                    nxt_bh, nxt_b = bh_nx, b_nx
                    ev_pw = [ev_bh, ev_b2, ev_da]
                    for lev in range(5):
                        lastlev = (lev == 4)
                        def f_sqr(eh, cur_bh=cur_bh, cur_b=cur_b, lastlev=lastlev):
                            last = None
                            for p in range(8):
                                cc = slice(p * CH, (p + 1) * CH)
                                a, bb = slice(0, 64), slice(64, 128)
                                last = eh.matmul(pb_gkk[a, cc], cur_b[a, cc], cur_bh[a, cc],
                                                 start=True, stop=True, tile_position=(0, 0))
                                last = eh.matmul(pb_gkk[bb, cc], cur_b[bb, cc], cur_bh[bb, cc],
                                                 start=True, stop=True, tile_position=(64, 64))
                                if not lastlev:
                                    last = eh.matmul(pb_gkq[a, cc], cur_bh[a, cc], cur_b[a, cc],
                                                     start=True, stop=True, tile_position=(0, 0))
                                    last = eh.matmul(pb_gkq[bb, cc], cur_bh[bb, cc],
                                                     cur_b[bb, cc], start=True, stop=True,
                                                     tile_position=(64, 64))
                            return last
                        sqr_waits = list(ev_pw)
                        if lev == 0:
                            sqr_waits += [prev_ap4]
                        ev_sqr = sc.add("t", f_sqr, sqr_waits, f"sqr{s_i}_{b}_{c}_{lev}")
                        if lev == 0:
                            prev_sqr0 = ev_sqr
                        def f_sqe(eh, nxt_bh=nxt_bh, nxt_b=nxt_b, lastlev=lastlev):
                            r = eh.activation(nxt_bh[:, 0:512], pb_gkk[:, 0:512], AF.Copy)
                            if lastlev:
                                return r
                            return eh.activation(nxt_b[:, 0:512], pb_gkq[:, 0:512], AF.Copy)
                        ev_sqe = sc.add("s", f_sqe, [ev_sqr], f"sqe{s_i}_{b}_{c}_{lev}")
                        if lastlev:
                            prev_sqe4 = ev_sqe
                        def f_ap(eh, nxt_bh=nxt_bh, vcur=vcur):
                            last = None
                            for p in range(8):
                                cc = slice(p * CH, (p + 1) * CH)
                                last = eh.matmul(pb_mm[0:64, cc], nxt_bh[0:64, cc],
                                                 vj[vcur][0:64, cc], start=True, stop=True,
                                                 tile_position=(0, 0))
                                last = eh.matmul(pb_mm[64:128, cc], nxt_bh[64:128, cc],
                                                 vj[vcur][64:128, cc], start=True, stop=True,
                                                 tile_position=(64, 64))
                            return last
                        ev_ap = sc.add("t", f_ap, [ev_sqe, ev_vj], f"ap{s_i}_{b}_{c}_{lev}")
                        if lastlev:
                            prev_ap4 = ev_ap
                        def f_ae(eh, vcur=vcur):
                            return eh.tensor_add(vj[1 - vcur][:, 0:512], vj[vcur][:, 0:512],
                                                 pb_mm[:, 0:512])
                        ev_vj = sc.add("v", f_ae, [ev_ap], f"ae{s_i}_{b}_{c}_{lev}")
                        vcur = 1 - vcur
                        cur_bh, nxt_bh = nxt_bh, cur_bh
                        cur_b, nxt_b = nxt_b, cur_b
                        ev_pw = [ev_sqe, ev_vj]
                    if chsub < 5:
                        stage_used[cp] = ev_qm
                        continue
                    def f_w(eh, vcur=vcur):
                        last = None
                        for p in range(8):
                            cc = slice(p * CH, (p + 1) * CH)
                            last = eh.tensor_scalar_mul(wsl[:, cc], vj[vcur][:, cc],
                                                        tokv[:, 1, p:p + 1])
                        return last
                    ev_w = sc.add("v", f_w, [ev_vj, ev_tve, prev_omm], f"w{s_i}_{b}_{c}")
                    if dbg_ch and "wsl" in dbg:
                        sc.gop(lambda eh, dma: dma(dbg_h["wsl"][:], wsl[:]), 1, [ev_w], "dbgw")

                    def f_omm(eh):
                        last = None
                        for p in range(8):
                            cc = slice(p * CH, (p + 1) * CH)
                            last = eh.matmul(pb_o[0:64, cc], S_bf[0:64, p, :], qhat[0:64, cc],
                                             start=True, stop=False, tile_position=(0, 0))
                            last = eh.matmul(pb_o[0:64, cc], wsl[0:64, cc], datile[0:64, cc],
                                             start=False, stop=True, tile_position=(0, 0))
                            last = eh.matmul(pb_o[64:128, cc], S_bf[64:128, p, :],
                                             qhat[64:128, cc], start=True, stop=False,
                                             tile_position=(64, 64))
                            last = eh.matmul(pb_o[64:128, cc], wsl[64:128, cc],
                                             datile[64:128, cc], start=False, stop=True,
                                             tile_position=(64, 64))
                        return last
                    ev_omm = sc.add("t", f_omm, [ev_w, ev_qh, ev_da, prev_oev], f"om{s_i}_{b}_{c}")
                    prev_omm = ev_omm
                    def f_oev(eh, csl=csl):
                        return eh.activation(ofm[:, :, csl],
                                             pb_o[:, 0:512].rearrange("p (c t) -> p c t", t=CH),
                                             AF.Copy)
                    ev_oev = sc.add("s", f_oev, [ev_omm], f"oe{s_i}_{b}_{c}")
                    prev_oev = ev_oev

                    def f_wp(eh):
                        last = None
                        for p in range(8):
                            cc = slice(p * CH, (p + 1) * CH)
                            last = eh.tensor_scalar_mul(wps[:, cc], wsl[:, cc],
                                                        tokv[:, 4, p:p + 1])
                        return last
                    ev_wp = sc.add("v", f_wp, [ev_w, ev_gc5, prev_smm], f"wp{s_i}_{b}_{c}")
                    def f_smm(eh):
                        last = None
                        for p in range(8):
                            cc = slice(p * CH, (p + 1) * CH)
                            last = eh.matmul(pb_mm[0:64, cc], ktok[0:64, p, :], wps[0:64, cc],
                                             start=True, stop=True, tile_position=(0, 0))
                            last = eh.matmul(pb_mm[64:128, cc], ktok[64:128, p, :],
                                             wps[64:128, cc], start=True, stop=True,
                                             tile_position=(64, 64))
                        return last
                    ev_smm = sc.add("t", f_smm, [ev_wp, ev_kte, ev_omm], f"sm{s_i}_{b}_{c}")
                    prev_smm = ev_smm
                    prev_smm_box[0] = ev_smm
                    def f_sup(eh):
                        last = None
                        for p in range(8):
                            cc = slice(p * CH, (p + 1) * CH)
                            last = eh.scalar_tensor_tensor(S_sb[:, p, :], S_sb[:, p, :],
                                                           tokv[:, 3, p:p + 1], pb_mm[:, cc],
                                                           OP.mult, OP.add)
                        return last
                    ev_sup0 = sc.add("v", f_sup, [ev_smm, ev_gc5], f"su{s_i}_{b}_{c}")
                    ev_Supd = sc.add("s", lambda eh: eh.activation(S_bf[:], S_sb[:], AF.Copy),
                                     [ev_sup0], f"sbf{s_i}_{b}_{c}")
                    stage_used[cp] = ev_qm

                if dbg_here and "ofm" in dbg:
                    sc.gop(lambda eh, dma: dma(dbg_h["ofm"][:],
                                               ofm[:].rearrange("p c t -> p (c t)")),
                           1, [ev_oev], "dbgofm")
                if stage < 5:
                    xfree_hist[xb] = ev_xn
                    prev_xt_guard = ev_lo
                    continue

                ev_osq = sc.add("s", lambda eh: eh.activation(sqscr[:, 0:D],
                                                              ofm[:].rearrange("p c t -> p (c t)"),
                                                              AF.Square),
                                [ev_oev, ev_qe], f"osq{s_i}_{b}")
                def f_ocs(eh):
                    last = None
                    for cc2 in range(8):
                        last = eh.matmul(pb_exp[0:1, 0:TB], onesb[:, 0:1],
                                         sqscr[:, cc2 * TB:(cc2 + 1) * TB],
                                         start=(cc2 == 0), stop=(cc2 == 7))
                    return last
                ev_ocs = sc.add("t", f_ocs, [ev_osq, ev_qm], f"ocs{s_i}_{b}")
                ev_ors = sc.add("s", lambda eh: eh.activation(rsc[0:1, 0:TB], pb_exp[0:1, 0:TB],
                                                              AF.Sqrt, bias=cvec[0:1, 1:2],
                                                              scale=1.0 / D),
                                [ev_ocs], f"ors{s_i}_{b}")
                def f_orr(eh):
                    with nc.allow_low_precision(reason="bf16 rstd feeds bf16 matmul"):
                        return eh.reciprocal(rsc[0:1, 0:TB], rsc[0:1, 0:TB])
                ev_orr = sc.add("v", f_orr, [ev_ors], f"orr{s_i}_{b}")
                ev_obc = sc.add("t", lambda eh: eh.matmul(pb_exp[:, 0:TB], onesb[0:1, 0:128],
                                                          rsc[0:1, 0:TB], start=True, stop=True),
                                [ev_orr], f"obc{s_i}_{b}")
                def f_onm(eh):
                    last = None
                    for cc2 in range(8):
                        last = eh.tensor_tensor(rfm[:, cc2, :], ofm[:, cc2, :], pb_exp[:, 0:TB],
                                                OP.mult)
                    return last
                ev_onm = sc.add("v", f_onm, [ev_obc], f"onm{s_i}_{b}")
                def f_wom(eh):
                    last = None
                    for half in range(2):
                        pbank = pb_p0 if half == 0 else pb_p1
                        for cc2 in range(8):
                            last = eh.matmul(pbank[:, 0:512], rfm[:, cc2, :],
                                             wo_sb[:, cc2, half * 512:(half + 1) * 512],
                                             start=(cc2 == 0), stop=(cc2 == 7))
                    return last
                ev_wom = sc.add("t", f_wom, [ev_onm, ev_wofold], f"wom{s_i}_{b}")
                def f_oadd(eh, x_slab=x_slab):
                    eh.tensor_add(out_slab[:, 0:512], pb_p0[:, 0:512], x_slab[:, 0:512])
                    return eh.tensor_add(out_slab[:, 512:1024], pb_p1[:, 0:512],
                                         x_slab[:, 512:1024])
                ev_oadd = sc.add("v", f_oadd, [ev_wom, ev_ost], f"oa{s_i}_{b}")
                prev_oadd = ev_oadd
                ev_ost = sc.gop(lambda eh, dma, xr0=xr0: dma(out_h[xr0:xr0 + 128, :],
                                                             out_slab[:]),
                                1, [ev_oadd], f"os{s_i}_{b}")
                xfree_hist[xb] = ev_ost
                prev_xt_guard = ev_lo

            if stage < 6:
                prev_b_done_v = ev_Supd
                prev_b_done_t = None
                continue

            def f_bwl(eh, dma):
                for c in range(8):
                    dma(wg_sb[:, c, :], wgT_h[c * 128:(c + 1) * 128, :])
                    dma(wu_sb[:, c, :], wuT_h[c * 128:(c + 1) * 128, :])
            ev_bwl = sc.gop(f_bwl, 16, [ev_wom, ev_Supd, ev_ost], f"bwl{s_i}")

            ev_fadd_prev = None
            ev_fst_prev = prev_seq_fst
            hp_evs = {}
            dn_evs = {}
            hhi_evs = {}
            hpm_evs = {}
            gu_last_of_gr = {}
            xlo_ld = {}
            ev_dn_last = None
            ngr = (n_blk * TB) // FFG
            for gr in range(ngr):
                g0 = gr * FFG
                def f_xll(eh, dma, g0=g0):
                    dma(xlo_buf[:, 0], xlo_h[:, :, g0:g0 + FFG])
                xlo_ld[gr] = sc.gop(f_xll, 1,
                                    [prev_xlo_st, ev_bwl, gu_last_of_gr.get(gr - 1)],
                                    f"xll{s_i}_{gr}")
                for hc in range(NHC):
                    gi = gr * NHC + hc
                    hb = hc % 2
                    gbank = pb_gkk if hb == 0 else pb_gkq
                    ubank = pb_exp if hb == 0 else pb_mm
                    def f_gu(eh, hc=hc, g0=g0, gr=gr, gbank=gbank, ubank=ubank):
                        last = None
                        for bank, w_sb_ in ((gbank, wg_sb), (ubank, wu_sb)):
                            for d in range(8):
                                last = eh.matmul(bank[:, 0:FFG],
                                                 w_sb_[:, d, hc * 128:(hc + 1) * 128],
                                                 xn_seq[:, d, g0:g0 + FFG],
                                                 start=(d == 0), stop=False)
                            for d in range(8):
                                last = eh.matmul(bank[:, 0:FFG],
                                                 w_sb_[:, d, hc * 128:(hc + 1) * 128],
                                                 xlo_buf[:, 0, d, :],
                                                 start=False, stop=(d == 7))
                        return last
                    waits = [ev_bwl, ev_xnw_last, hp_evs.get(gi - 2), xlo_ld[gr]]
                    ev_gu = sc.add("t", f_gu, waits, f"gu{s_i}_{gr}_{hc}")
                    gu_last_of_gr[gr] = ev_gu
                    ev_si = sc.add("s", lambda eh, gbank=gbank: eh.activation(
                        sil[:, 0:FFG], gbank[:, 0:FFG], AF.Silu, scale=INV127),
                        [ev_gu, hpm_evs.get(gi - 1)], f"si{s_i}_{gr}_{hc}")
                    def f_hp(eh, ubank=ubank):
                        return eh.scalar_tensor_tensor(hh32[:, 0, :], ubank[:, 0:FFG],
                                                       INV127SQ, sil[:, 0:FFG],
                                                       OP.mult, OP.mult)
                    ev_hpm = sc.add("v", f_hp, [ev_si, hhi_evs.get(gi - 1)],
                                    f"hpm{s_i}_{gr}_{hc}")
                    hpm_evs[gi] = ev_hpm
                    ev_hhi = sc.add("s", lambda eh, hb=hb: eh.activation(
                        hp_hi[:, hb, :], hh32[:, 0, :], AF.Copy),
                        [ev_hpm, dn_evs.get(gi - 2)], f"hhi{s_i}_{gr}_{hc}")
                    hhi_evs[gi] = ev_hhi
                    def f_hlo(eh, hb=hb):
                        return eh.tensor_sub(hp_lo[:, hb, :], hh32[:, 0, :], hp_hi[:, hb, :])
                    ev_hlo = sc.add("v", f_hlo, [ev_hhi, dn_evs.get(gi - 2)],
                                    f"hlo{s_i}_{gr}_{hc}")
                    hp_evs[gi] = ev_hlo
                    def f_wdl(eh, dma, hc=hc, hb=hb):
                        dma(wdbuf[:, hb, :], wdT_h[hc * 128:(hc + 1) * 128, :])
                    ev_wdl = sc.gop(f_wdl, 1, [dn_evs.get(gi - 2), ev_bwl],
                                    f"wdl{s_i}_{gr}_{hc}")
                    def f_dn(eh, hc=hc, hb=hb):
                        last = None
                        for pi, hp_t in enumerate((hp_hi, hp_lo)):
                            st = (hc == 0 and pi == 0)
                            sp = (hc == NHC - 1 and pi == 1)
                            for bank, ssl in ((pb_p0, slice(0, 128)), (pb_q0, slice(128, 256))):
                                last = eh.matmul(bank[:, 0:512], hp_t[:, hb, ssl],
                                                 wdbuf[:, hb, 0:512], start=st, stop=sp)
                            for bank, ssl in ((pb_p1, slice(0, 128)), (pb_q1, slice(128, 256))):
                                last = eh.matmul(bank[:, 0:512], hp_t[:, hb, ssl],
                                                 wdbuf[:, hb, 512:1024], start=st, stop=sp)
                        return last
                    dn_waits = [ev_wdl, ev_hlo]
                    if hc == 0:
                        dn_waits.append(ev_fadd_prev if ev_fadd_prev else ev_bwl)
                    ev_dn = sc.add("t", f_dn, dn_waits, f"dn{s_i}_{gr}_{hc}")
                    dn_evs[gi] = ev_dn
                    ev_dn_last = ev_dn
                for sl_i in range(FFG // 128):
                    xr0 = s_i * SEQ + gr * FFG + sl_i * 128
                    ba, bb2 = (pb_p0, pb_p1) if sl_i == 0 else (pb_q0, pb_q1)
                    ev_opl = sc.gop(lambda eh, dma, xr0=xr0: dma(out_slab[:],
                                                                 out_h[xr0:xr0 + 128, :]),
                                    1, [ev_fadd_prev, ev_fst_prev, ev_bwl],
                                    f"opl{s_i}_{gr}_{sl_i}")
                    def f_fadd(eh, ba=ba, bb2=bb2):
                        eh.tensor_add(out_slab[:, 0:512], ba[:, 0:512], out_slab[:, 0:512])
                        return eh.tensor_add(out_slab[:, 512:1024], bb2[:, 0:512],
                                             out_slab[:, 512:1024])
                    ev_fadd = sc.add("v", f_fadd, [ev_dn_last, ev_opl], f"fa{s_i}_{gr}_{sl_i}")
                    ev_fst = sc.gop(lambda eh, dma, xr0=xr0: dma(out_h[xr0:xr0 + 128, :],
                                                                 out_slab[:]),
                                    1, [ev_fadd], f"fs{s_i}_{gr}_{sl_i}")
                    ev_fadd_prev = ev_fadd
                    ev_fst_prev = ev_fst

            prev_b_done_v = ev_fadd_prev
            prev_b_done_t = ev_dn_last
            prev_seq_fst = ev_fst_prev

        if reorder:
            sc.reorder()
        sc.finalize()
        sc.emit(block, sems)
        nc._mk_sched = sc
        if verbose:
            print(f"[mk] ops={len(sc.ops)} counts={sc.final_counts} sbuf_rem={rem}")
    return nc


def prep_inputs(inp, core, n_cores=8, n_seq=2):
    P = {}
    x = inp["L0_attn_x"]
    b0 = core * n_seq
    P["x"] = np.ascontiguousarray(x[b0:b0 + n_seq].reshape(n_seq * SEQ, D))
    P["wq"] = np.ascontiguousarray(inp["L0_attn_Wq"])
    P["wk"] = np.ascontiguousarray(inp["L0_attn_Wk"])
    P["wv"] = np.ascontiguousarray(inp["L0_attn_Wv"])
    P["wo"] = np.ascontiguousarray(inp["L0_attn_Wo"])
    wr2 = np.zeros((D, 2 * D), np.float32)
    wr2r = wr2.reshape(D, 2, H, DH)
    wr2r[:, 0, :, 0:48] = inp["L0_attn_Wrq"].reshape(D, H, 48)
    wr2r[:, 1, :, 0:48] = inp["L0_attn_Wrk"].reshape(D, H, 48)
    P["wr2"] = np.ascontiguousarray(wr2)
    perm = [2 * i for i in range(8)] + [2 * i + 1 for i in range(8)]
    wfl = np.concatenate([inp["L0_attn_Wf"][:, perm], inp["L0_attn_Wl"][:, perm]], axis=1)
    P["wfl"] = np.ascontiguousarray(wfl.astype(np.float32))
    bfl = np.concatenate([np.asarray(inp["L0_attn_bf"])[perm],
                          np.asarray(inp["L0_attn_bl"])[perm]])
    P["bfl"] = np.ascontiguousarray(bfl[None, :].astype(np.float32))
    P["convw"] = np.ascontiguousarray(np.concatenate(
        [inp["L0_attn_conv_q"], inp["L0_attn_conv_k"], inp["L0_attn_conv_v"]], axis=1))
    P["qn2"] = np.ascontiguousarray(np.tile(inp["L0_attn_qn_w"], 2)[:, None].astype(np.float32))
    P["kn2"] = np.ascontiguousarray(np.tile(inp["L0_attn_kn_w"], 2)[:, None].astype(np.float32))
    P["onw8"] = np.ascontiguousarray(
        np.asarray(inp["L0_attn_on_w"]).reshape(8, 128).T.astype(np.float32))
    gs = np.asarray(inp["L0_attn_graph_scale"])
    gs8 = np.empty((128, 8), np.float32)
    for p in range(8):
        gs8[0:64, p] = gs[2 * p]
        gs8[64:128, p] = gs[2 * p + 1]
    P["gs8"] = gs8
    import ml_dtypes
    P["wgT"] = np.ascontiguousarray(inp["L0_ffn_w_gate"].T.astype(ml_dtypes.bfloat16))
    P["wuT"] = np.ascontiguousarray(inp["L0_ffn_w_up"].T.astype(ml_dtypes.bfloat16))
    P["wdT"] = np.ascontiguousarray(inp["L0_ffn_w_down"].T.astype(ml_dtypes.bfloat16))
    P["c_ident"] = np.eye(128, dtype=np.float32)
    tt = np.arange(CH)
    m = np.where(tt[:, None] < tt[None, :], 0.0, -1e30).astype(np.float32)
    P["c_maskbh"] = np.ascontiguousarray(np.tile(m, (2, 8)))
    mb = np.where(tt[:, None] > tt[None, :], 0.0, -1e30).astype(np.float32)
    P["c_maskb"] = np.ascontiguousarray(np.tile(mb, (2, 8)))
    ma = np.where(tt[:, None] <= tt[None, :], 0.0, -1e30).astype(np.float32)
    P["c_maska"] = np.ascontiguousarray(np.tile(ma, (2, 8)))
    P["c_ones"] = np.ones((128, 512), np.float32)
    cv = np.zeros((128, 8), np.float32)
    cv[:, 0] = 1e-5
    cv[:, 1] = 1e-6
    cv[:, 2] = 1e-12
    cv[:, 3] = 1.0
    P["c_cvec"] = cv
    return P


def kernel(**inputs):
    from concourse.bass_utils import run_bass_kernel_spmd
    n_cores = 8
    percore = [prep_inputs(inputs, core) for core in range(n_cores)]
    nc = build(verbose=False)
    r = run_bass_kernel_spmd(nc, percore, core_ids=list(range(n_cores)))
    outs = [np.asarray(r.results[c]["out"], np.float32).reshape(2, SEQ, D)
            for c in range(n_cores)]
    return np.concatenate(outs, axis=0)
